# version 13
# baseline (speedup 1.0000x reference)
import numpy as np
import ml_dtypes
import concourse.bass as bass
import concourse.mybir as mybir
from concourse.bass import IndirectOffsetOnAxis
from concourse.bass_utils import run_bass_kernel_spmd
from contextlib import ExitStack

F32 = mybir.dt.float32
BF16 = mybir.dt.bfloat16
I32 = mybir.dt.int32
AF = mybir.ActivationFunctionType
ALU = mybir.AluOpType
AX = mybir.AxisListType.X
BF = ml_dtypes.bfloat16

NCORE = 8
T = 2048
NT = 16384
E = 1024
H = 16
D = 64
FF = 4096
CAP = 2048
FBPC = 1024
FBG = NCORE * FBPC
ROWS = CAP + FBPC
NCH = 16
TBLN = NT + FBG
LGW = 1e30

ENGS = ("sync", "scalar", "vector", "tensor", "gpsimd")


class Bld:
    def __init__(self):
        self.q = {e: [] for e in ENGS}
        self.cnt = {}

    def sem(self, *names):
        for n in names:
            self.cnt[n] = 0

    def op(self, eng, fn, waits=(), inc=None, inc_by=1):
        waits = list(waits)
        self_sem = {"vector": "cV", "scalar": "cA"}.get(eng)
        if self_sem is not None:
            waits.append((self_sem, self.cnt[self_sem]))
        merged = {}
        for s, v in waits:
            if s.startswith("q"):
                v = self.cnt[s]
            merged[s] = max(merged.get(s, 0), v)
        self.q[eng].append((tuple(merged.items()), fn, inc, inc_by))
        if inc is not None:
            self.cnt[inc] += inc_by

    def n(self, sem):
        return self.cnt[sem]


def _build(dump=None):
    dump = dump or []
    nc = bass.Bass("TRN2", target_bir_lowering=False, debug=False)

    def din(name, shape, dt=F32):
        return nc.dram_tensor(name, shape, dt, kind="ExternalInput").ap()

    x_in = din("x", [T, E])
    wq = din("wq", [E, E]); wk = din("wk", [E, E]); wv = din("wv", [E, E])
    wqp = din("wqp", [E, E]); wkp = din("wkp", [E, E]); wo = din("wo", [E, E])
    wg1 = din("wg1", [E, 64]); wg2 = din("wg2", [64, E])
    gkc = din("gkc", [128, 8]); bkc = din("bkc", [128, 8])
    rwt = din("rwt", [E, 8]); rb = din("rb", [1, 8])
    w1m = din("w1m", [E, FF]); w2m = din("w2m", [FF, E])
    sw1 = din("sw1", [E, FF]); sw2 = din("sw2", [FF, E])
    b1s = din("b1s", [128, 64])
    b2x = din("b2x", [1, 2 * E])
    csin = din("csin", [96, T])
    pmask = din("pmask", [1, 8])
    intidx = din("intidx", [128, 8], I32)
    intidx2 = din("intidx2", [128, 24], I32)
    tids = din("tids", [128, 16], I32)

    out_t = nc.dram_tensor("out", [T, E], F32, kind="ExternalOutput").ap()

    def dint(name, shape, dt=F32):
        kind = "ExternalOutput" if name in dump else "Internal"
        return nc.dram_tensor(name, shape, dt, kind=kind).ap()

    xnag_in = dint("xnag_in", [T, E], BF16)
    xnall = nc.dram_tensor("xnall", [NT, E], BF16, kind="Internal").ap()
    lg_in = dint("lg_in", [T, 8])
    lg_all = nc.dram_tensor("lg_all", [NT, 8], F32, kind="Internal").ap()
    sl_in = dint("sl_in", [H * D, D])
    sl_all = nc.dram_tensor("sl_all", [NCORE * H * D, D], F32, kind="Internal").ap()
    tbl_in = nc.dram_tensor("tbl_in", [TBLN, 1], I32, kind="Internal").ap()
    tbl_all = nc.dram_tensor("tbl_all", [TBLN, 1], I32, kind="Internal").ap()
    ymine = dint("ymine", [ROWS, E], BF16)
    yall = nc.dram_tensor("yall", [NCORE * ROWS, E], BF16,
                          kind="Internal").ap()
    sw1d = nc.dram_tensor("sw1d", [E, FF], BF16, kind="Internal").ap()
    sw2d = nc.dram_tensor("sw2d", [FF, E], BF16, kind="Internal").ap()
    wkd = nc.dram_tensor("wkd", [E, E], BF16, kind="Internal").ap()
    wkpd = nc.dram_tensor("wkpd", [E, E], BF16, kind="Internal").ap()
    wvd = nc.dram_tensor("wvd", [E, E], BF16, kind="Internal").ap()
    wqd = nc.dram_tensor("wqd", [E, E], BF16, kind="Internal").ap()
    wqpd = nc.dram_tensor("wqpd", [E, E], BF16, kind="Internal").ap()
    wod = nc.dram_tensor("wod", [E, E], BF16, kind="Internal").ap()
    wg1d = nc.dram_tensor("wg1d", [E, 64], BF16, kind="Internal").ap()
    wg2d = nc.dram_tensor("wg2d", [64, E], BF16, kind="Internal").ap()
    w1d = nc.dram_tensor("w1d", [E, FF], BF16, kind="Internal").ap()
    w2d = nc.dram_tensor("w2d", [FF, E], BF16, kind="Internal").ap()
    spark = dint("spark", [T, E])
    idxall = dint("idxall", [NT, 1], I32)
    idxy = dint("idxy", [NT, 1], I32)
    bnc = nc.dram_tensor("bnc", [8, E], F32, kind="Internal").ap()
    ogd = nc.dram_tensor("ogd", [T, E], F32, kind="Internal").ap()
    attd = nc.dram_tensor("attd", [T, E], BF16, kind="Internal").ap()
    dbgd = {}
    for nm, shp, dt in (("d_xnall", [NT, E], BF16), ("d_lgall", [NT, 8], F32),
                        ("d_slall", [NCORE * H * D, D], F32),
                        ("d_tbl", [TBLN, 1], I32),
                        ("d_yall", [NCORE * ROWS, E], BF16)):
        if nm in dump:
            dbgd[nm] = nc.dram_tensor(nm, shp, dt, kind="ExternalOutput").ap()

    c_idbf = nc.inline_tensor(np.eye(128, dtype=BF), "c_idbf").ap()
    c_idf = nc.inline_tensor(np.eye(128, dtype=np.float32), "c_idf").ap()
    c_mask = nc.inline_tensor(np.triu(np.ones((128, 128), np.float32)), "c_mask").ap()
    c_trub = nc.inline_tensor(np.triu(np.ones((128, 128), BF), 1), "c_trub").ap()
    c_truf = nc.inline_tensor(np.triu(np.ones((128, 128), np.float32), 1),
                              "c_truf").ap()
    c_onecb = nc.inline_tensor(np.ones((128, 1), BF), "c_onecb").ap()
    c_onecf = nc.inline_tensor(np.ones((128, 1), np.float32), "c_onecf").ap()
    c_onerf = nc.inline_tensor(np.ones((1, 128), np.float32), "c_onerf").ap()
    c_zcb = nc.inline_tensor(np.zeros((1, 128), BF), "c_zcb").ap()
    c_onerb = nc.inline_tensor(np.ones((1, 512), BF), "c_onerb").ap()
    c_zroi = nc.inline_tensor(np.zeros((128, TBLN // 128), np.int32), "c_zroi").ap()
    c_vals = nc.inline_tensor(
        np.tile(np.array([[1e-5, -0.5, 0.0, 1.0, 2.0, 0.5, -1.0, 9.9e5]],
                         np.float32), (128, 1)), "c_vals").ap()

    B = Bld()
    B.sem("qS", "qA", "qG", "qI", "qO", "cP", "cV", "cA", "cG", "cc",
          "qL", "qXN", "qSL", "qY", "qJ", "qX0", "qX1", "qD0", "qD1",
          "qW0", "qW1", "qYA", "qYB", "qYC", "qZ0", "qZ1")
    bc_regs = {}

    def dma(eng, out, in_, sem, waits=(), transpose=False):
        if transpose:
            B.op(eng, (lambda o=out, i=in_: lambda e: e.dma_start(
                out=o, in_=i, transpose=True))(), waits=waits, inc=sem, inc_by=16)
        else:
            B.op(eng, (lambda o=out, i=in_: lambda e: e.dma_start(
                out=o, in_=i))(), waits=waits, inc=sem, inc_by=16)
        return B.n(sem)

    with ExitStack() as ctx:
        en = ctx.enter_context

        def sb(name, shape, dt):
            return en(nc.sbuf_tensor(name, shape, dt))

        PS_A = en(nc.psum_tensor("PS_A", [128, 512], F32))
        PS_B = en(nc.psum_tensor("PS_B", [128, 512], F32))
        PS_C = en(nc.psum_tensor("PS_C", [128, 512], F32))
        PS_E = en(nc.psum_tensor("PS_E", [128, 1024], F32))
        PS_F = en(nc.psum_tensor("PS_F", [128, 1024], F32))
        PS_TB = en(nc.psum_tensor("PS_TB", [128, 1024], BF16))

        ABT = sb("ABT", [128, 86016], BF16)
        AFT = sb("AFT", [128, 8192], F32)
        QFM = ABT[:, 0:16384]
        KFM = ABT[:, 16384:32768]
        KTM = ABT[:, 32768:49152]
        VTM = ABT[:, 49152:65536]
        WBUF = ABT[:, 65536:81920]
        XNTF = ABT[:, 32768:49152]
        WG1S = ABT[:, 81920:82432]
        WOB = ABT[:, 65536:73728]
        XNB = ABT[:, 0:2048]
        RB16 = ABT[:, 84480:85504]
        SPB0 = ABT[:, 73728:74752]
        SPB1 = ABT[:, 74752:75776]
        STS = ABT[:, 75776:76800]
        SOP = ABT[:, 76800:77824]
        G1FM = ABT[:, 82432:84480]
        ETM = ABT[:, 79872:80384]
        WG2S = ABT[:, 80384:81408]
        W1B = ABT[:, 32768:65536]
        W2B = ABT[:, 0:32768]
        HSB = ABT[:, 65536:81920]
        XG = ABT[:, 81920:86016]
        XGR = ABT[:, 65536:69632]
        XT = AFT[:, 0:2048]
        XNT = AFT[:, 2048:4096]
        TABS = AFT[:, 0:1024]
        TABSB = AFT[:, 3072:4096]
        TQ = AFT[:, 1024:1536]
        TQP = AFT[:, 1536:2048]
        LG = AFT[:, 0:1024]
        R1 = AFT[:, 1024:2048]
        R2 = AFT[:, 2048:3072]
        R3 = AFT[:, 3072:4096]
        R4 = AFT[:, 4096:5120]
        R5 = AFT[:, 5120:6144]
        RS = AFT[:, 6144:7680]
        GATE = AFT[:, 0:1024]
        GATE1 = AFT[:, 6144:7168]
        OG = AFT[:, 1024:2048]
        SPRE = AFT[:, 2048:3072]
        SOUT = AFT[:, 3072:5120]
        SPREF = AFT[:, 5120:6144]
        XLP = AFT[:, 4096:5120]
        YEV = AFT[:, 0:2048]
        B2BC = AFT[:, 2048:4096]
        B2ROW = AFT[0:1, 4096:6144]
        BWOB0 = ABT[0:1, 81920:82432]
        BWOB1 = ABT[0:1, 85504:86016]
        SST = AFT[:, 7168:8192]
        YGB = ABT[:, 0:2048]
        SLD = AFT[:, 2048:4096]
        XLD = AFT[:, 4096:6144]
        OUTT = AFT[:, 6144:8192]

        RWS = sb("RWS", [128, 64], F32)
        RBBC = sb("RBBC", [128, 8], F32)
        PMB = sb("PMB", [128, 8], F32)
        GKC = sb("GKC", [128, 8], F32)
        BKC = sb("BKC", [128, 8], F32)
        B1SS = sb("B1SS", [128, 64], F32)
        IIX = sb("IIX", [128, 8], I32)
        IIX2 = sb("IIX2", [128, 24], I32)
        TIDS = sb("TIDS", [128, 16], I32)
        SMALL = sb("SMALL", [128, 48], F32)
        RCE = sb("RCE", [128, 64], F32)
        IDXI = sb("IDXI", [128, 128], I32)
        OWNW = sb("OWNW", [128, 16], I32)
        OWNY = sb("OWNY", [128, 16], I32)
        TBLW = sb("TBLW", [128, 24], I32)
        TBLF = sb("TBLF", [128, 24], F32)
        IDBF = sb("IDBF", [128, 128], BF16)
        IDF = sb("IDF", [128, 128], F32)
        MSK = sb("MSK", [128, 128], F32)
        TRB = sb("TRB", [128, 128], BF16)
        TRF = sb("TRF", [128, 128], F32)
        OCB = sb("OCB", [128, 1], BF16)
        OCF = sb("OCF", [128, 1], F32)
        ORF = sb("ORF", [1, 128], F32)
        ZCB = sb("ZCB", [1, 128], BF16)
        ORB = sb("ORB", [1, 512], BF16)
        RBROW = sb("RBROW", [1, 8], F32)
        CASB = sb("CASB", [128, 8], F32)
        BKCB = sb("BKCB", [128, 8], BF16)
        PMROW = sb("PMROW", [1, 8], F32)

        for i_, v_ in enumerate([1e-5, -0.5, 0.0, 1.0, 2.0, 0.5, -1.0, 9.9e5]):
            nc.const_aps.register(F32, v_, CASB[:, i_:i_ + 1])

        semh = {}
        for s_ in ("qS", "qA", "qG", "qI", "qO", "cP", "cV", "cA", "cG", "cc",
                   "qL", "qXN", "qSL", "qY", "qJ", "qX0", "qX1", "qD0", "qD1",
                   "qW0", "qW1", "qYA", "qYB", "qYC", "qZ0", "qZ1"):
            semh[s_] = en(nc.semaphore(s_))

        for dst, src in ((IIX[:, :], intidx), (IIX2[:, :], intidx2),
                         (TIDS[:, :], tids),
                         (B1SS[:, :], b1s), (IDBF[:, :], c_idbf),
                         (IDF[:, :], c_idf), (MSK[:, :], c_mask),
                         (TRB[:, :], c_trub), (TRF[:, :], c_truf),
                         (OCB[:, :], c_onecb), (OCF[:, :], c_onecf),
                         (ORF[:, :], c_onerf), (ZCB[:, :], c_zcb),
                         (ORB[:, :], c_onerb), (RBROW[:, :], rb),
                         (PMROW[:, :], pmask), (GKC[:, :], gkc),
                         (BKC[:, :], bkc), (CASB[:, :], c_vals)):
            dma("sync", dst, src, "qS")
        dma("sync", RWS.rearrange("p (kc e) -> p kc e", e=8),
            rwt.rearrange("(kc p) e -> p kc e", p=128), "qS")
        dma("sync", tbl_in.rearrange("(p f) one -> p (f one)", p=128), c_zroi, "qS")
        _pre_qS = B.n("qS")
        _pre_qG = B.n("qG")
        _wb = {}
        for nm, d_, s_ in (("wk", wkd, wk), ("wkp", wkpd, wkp),
                           ("wv", wvd, wv), ("wq", wqd, wq),
                           ("wqp", wqpd, wqp), ("wo", wod, wo),
                           ("wg1", wg1d, wg1), ("wg2", wg2d, wg2),
                           ("w1m", w1d, w1m), ("w2m", w2d, w2m),
                           ("sw1", sw1d, sw1), ("sw2", sw2d, sw2)):
            dma("gpsimd", d_, s_, "qG")
            _wb[nm] = B.n("qG")
        _swd_qG = B.n("qG")
        for dst, src in ((RBBC, RBROW[:, 0:8]), (PMB, PMROW[:, 0:8])):
            B.op("tensor", (lambda s=src: lambda t: t.matmul(
                PS_A[:, 0:8], ORF[:, :], s, start=True, stop=True))(),
                waits=[("qS", _pre_qS), ("cV", B.n("cV"))], inc="cP")
            B.op("vector", (lambda d=dst: lambda v: v.tensor_copy(
                d[:, 0:8], PS_A[:, 0:8]))(),
                waits=[("cP", B.n("cP"))], inc="cV")
        _bc_cV = B.n("cV")
        _bc_cP = B.n("cP")

        XNT3 = XNTF.rearrange("p (kc t) -> p kc t", t=2048)
        PT3B = PS_TB.rearrange("p (kc n) -> p kc n", n=128)
        x_tiles = x_in.rearrange("(c p) e -> c p e", p=128)
        xnag_tiles = xnag_in.rearrange("(c p) e -> c p e", p=128)
        lg_tiles = lg_in.rearrange("(c p) e -> c p e", p=128)
        ph1_v = {}
        ph1_p = {}
        ph1_a = {}
        for c in range(NCH):
            s0 = (c % 2) * 1024
            so = (c % 2) * 16
            sl = slice(s0, s0 + 1024)
            if c >= 2:
                wg_ = [("cV", ph1_v[c - 2]), ("cP", ph1_p[c - 2]),
                       ("qXN", ph1_a[c - 2])]
            else:
                wg_ = [("cV", _bc_cV), ("cP", _bc_cP)]
            qx = "qX0" if c % 2 == 0 else "qX1"
            _xq = dma("sync", XT[:, sl], x_tiles[c, :, :], qx, waits=wg_)
            B.op("vector", (lambda s_=sl, o_=so: lambda v: v.reduce_sum(
                SMALL[:, o_:o_ + 1], XT[:, s_], axis=AX))(),
                waits=[(qx, _xq)], inc="cV")
            B.op("scalar", (lambda s_=sl, o_=so: lambda s: s.activation(
                XNT[:, s_], XT[:, s_], AF.Square,
                accum_out=SMALL[:, o_ + 1:o_ + 2]))(),
                waits=[(qx, _xq), ("cV", B.n("cV"))], inc="cA")
            B.op("vector", (lambda o_=so: lambda v: v.tensor_scalar(
                SMALL[:, o_ + 2:o_ + 3], SMALL[:, o_:o_ + 1],
                1.0 / E, None, ALU.mult))(),
                waits=[("cA", B.n("cA"))], inc="cV")
            B.op("vector", (lambda o_=so: lambda v: v.tensor_mul(
                SMALL[:, o_ + 3:o_ + 4], SMALL[:, o_ + 2:o_ + 3],
                SMALL[:, o_ + 2:o_ + 3]))(), inc="cV")
            B.op("vector", (lambda o_=so: lambda v: v.tensor_scalar(
                SMALL[:, o_ + 4:o_ + 5], SMALL[:, o_ + 1:o_ + 2],
                1.0 / E, None, ALU.mult))(), inc="cV")
            B.op("vector", (lambda o_=so: lambda v: v.tensor_sub(
                SMALL[:, o_ + 4:o_ + 5], SMALL[:, o_ + 4:o_ + 5],
                SMALL[:, o_ + 3:o_ + 4]))(), inc="cV")
            B.op("scalar", (lambda o_=so: lambda s: s.activation(
                SMALL[:, o_ + 5:o_ + 6], SMALL[:, o_ + 4:o_ + 5],
                AF.Ln, bias=1e-5, scale=1.0))(),
                waits=[("cV", B.n("cV"))], inc="cA")
            B.op("scalar", (lambda o_=so: lambda s: s.activation(
                SMALL[:, o_ + 6:o_ + 7], SMALL[:, o_ + 5:o_ + 6],
                AF.Exp, bias=0.0, scale=-0.5))(), inc="cA")
            B.op("vector", (lambda s_=sl, o_=so: lambda v: v.tensor_scalar(
                XNT[:, s_], XT[:, s_], SMALL[:, o_ + 2:o_ + 3],
                SMALL[:, o_ + 6:o_ + 7], ALU.subtract, ALU.mult))(),
                waits=[("cA", B.n("cA"))], inc="cV")
            _xn_cV = B.n("cV")
            B.op("vector", (lambda s_=sl: lambda v: v.tensor_copy(
                XNB[:, s_], XNT[:, s_]))(), inc="cV")
            dma("scalar", xnag_tiles[c, :, :], XNB[:, sl], "qXN",
                waits=[("cV", B.n("cV"))])
            for j in range(8):
                B.op("tensor", (lambda s_=s0, j_=j: lambda t: t.transpose(
                    PS_TB[:, j_ * 128:j_ * 128 + 128],
                    XNB[:, s_ + j_ * 128:s_ + j_ * 128 + 128], IDBF[:, :]))(),
                    waits=([("cV", B.n("cV"))] if j == 0 else []),
                    inc=("cP" if j == 7 else None))
            B.op("vector", (lambda c_=c: lambda v: v.tensor_copy(
                XNT3[:, :, c_ * 128:c_ * 128 + 128], PT3B[:, :, :]))(),
                waits=[("cP", B.n("cP"))], inc="cV")
            rs_cv = {}
            for g in range(2):
                for j4 in range(4):
                    j = g * 4 + j4
                    B.op("tensor", (lambda s_=sl, j_=j: lambda t: t.transpose(
                        PS_B[:, (j_ % 4) * 128:(j_ % 4) * 128 + 128],
                        XNT[:, s_.start + j_ * 128:s_.start + j_ * 128 + 128],
                        IDF[:, :]))(),
                        waits=([("cV", max(_xn_cV, B.n("cV")))]
                               if j4 == 0 else []),
                        inc=("cP" if j4 == 3 else None))
                B.op("vector", (lambda g_=g: lambda v: v.tensor_copy(
                    RS[:, g_ * 512:g_ * 512 + 512], PS_B[:, 0:512]))(),
                    waits=[("cP", B.n("cP"))], inc="cV")
                rs_cv[g] = B.n("cV")
                for j4 in range(4):
                    j = g * 4 + j4
                    B.op("tensor", (lambda j_=j: lambda t: t.matmul(
                        PS_C[:, 0:8], RS[:, j_ * 128:j_ * 128 + 128],
                        RWS[:, j_ * 8:j_ * 8 + 8],
                        start=(j_ == 0), stop=(j_ == 7)))(),
                        waits=([("cV", rs_cv[g])] if j4 == 0 else []),
                        inc=("cP" if j == 7 else None))
            B.op("vector", lambda v: v.scalar_tensor_tensor(
                RS[:, 1024:1032], PS_C[:, 0:8], 1.0, RBBC[:, 0:8],
                ALU.mult, ALU.add),
                waits=[("cP", B.n("cP")), ("qL", B.n("qL"))], inc="cV")
            dma("scalar", lg_tiles[c, :, :], RS[:, 1024:1032], "qL",
                waits=[("cV", B.n("cV"))])
            ph1_v[c] = B.n("cV")
            ph1_p[c] = B.n("cP")
            ph1_a[c] = B.n("qXN")

        wk_ok = ("qW0", dma("sync",
                 WBUF[:, 0:8192].rearrange("p (kc f) -> p kc f", f=1024),
                 wkd.rearrange("(kc p) f -> p kc f", p=128), "qW0",
                 waits=[("qG", _wb["wk"])]))
        wkp_ok = ("qW1", dma("scalar",
                  WBUF[:, 8192:16384].rearrange("p (kc f) -> p kc f", f=1024),
                  wkpd.rearrange("(kc p) f -> p kc f", p=128), "qW1",
                  waits=[("qG", _wb["wkp"])]))
        B.op("gpsimd", lambda g: g.collective_compute(
            "AllGather", ALU.bypass, replica_groups=[list(range(NCORE))],
            ins=[lg_in], outs=[lg_all]),
            waits=[("qL", B.n("qL"))], inc="cc")
        _cc_lg = B.n("cc")
        B.op("gpsimd", lambda g: g.collective_compute(
            "AllGather", ALU.bypass, replica_groups=[list(range(NCORE))],
            ins=[xnag_in], outs=[xnall]),
            waits=[("qXN", B.n("qXN"))], inc="cc")
        _cc_xn = B.n("cc")

        def w_into(slot, srcd, waits, bk):
            waits = list(waits) + [("qG", _wb[bk])]
            q_, s_ = (("sync", "qW0") if slot == 0 else ("scalar", "qW1"))
            return (s_, dma(q_,
                    WBUF[:, slot * 8192:(slot + 1) * 8192].rearrange(
                        "p (kc f) -> p kc f", f=1024),
                    srcd.rearrange("(kc p) f -> p kc f", p=128), s_,
                    waits=waits))

        TABSD = (TABS, TABSB)

        def load_tabs(nc_i, waits, buf=0):
            last = 0
            s_ = slice(nc_i * 512, (nc_i + 1) * 512)
            for i, (qtr, r0, col) in enumerate(
                    [(0, 0, 0), (1, 0, 0), (2, 0, 0), (3, 0, 0),
                     (0, 32, 512), (1, 64, 512), (2, 32, 512), (3, 64, 512)]):
                last = dma("sync",
                           TABSD[buf][qtr * 32:qtr * 32 + 32, col:col + 512],
                           csin[r0:r0 + 32, s_], "qS",
                           waits=(waits if i == 0 else ()))
            return last

        def fm_pass(dst, wqg_pair, slots=(0, 1)):
            tok = {0: load_tabs(0, [("cV", B.n("cV"))], buf=0)}
            for nc_i in range(4):
                if nc_i + 1 < 4:
                    tok[nc_i + 1] = load_tabs(nc_i + 1, [("cV", B.n("cV"))],
                                              buf=(nc_i + 1) % 2)
                tb_b = TABSD[nc_i % 2]
                for ft in range(8):
                    for ps, wsl, tmp in ((PS_A, slots[0], TQ),
                                         (PS_B, slots[1], TQP)):
                        for kc in range(8):
                            B.op("tensor",
                                 (lambda p_=ps, w_=wsl, k_=kc, f_=ft,
                                  n_=nc_i: lambda t:
                                  t.matmul(p_[:, 0:512],
                                           WBUF[:, w_ * 8192 + k_ * 1024 + f_ * 128:
                                                w_ * 8192 + k_ * 1024 + f_ * 128 + 128],
                                           XNTF[:, k_ * 2048 + n_ * 512:
                                                k_ * 2048 + n_ * 512 + 512],
                                           start=(k_ == 0), stop=(k_ == 7)))(),
                                 waits=([wqg_pair[0], wqg_pair[1],
                                         ("cA", B.n("cA")),
                                         ("cV", B.n("cV"))] if kc == 0 else []),
                                 inc=("cP" if kc == 7 else None))
                        B.op("scalar",
                             (lambda p_=ps, t_=tmp: lambda s: s.activation(
                                 t_[:, :], p_[:, 0:512], AF.Silu))(),
                             waits=[("cP", B.n("cP")), ("cV", B.n("cV"))],
                             inc="cA")
                    dsl = dst[:, ft * 2048 + nc_i * 512:ft * 2048 + nc_i * 512 + 512]
                    B.op("vector", (lambda t_=tb_b: lambda v: v.tensor_mul(
                        TQ[:, :], TQ[:, :], t_[:, 0:512]))(),
                        waits=[("cA", B.n("cA")), ("qS", tok[nc_i])], inc="cV")
                    B.op("vector", (lambda t_=tb_b: lambda v: v.tensor_mul(
                        TQP[:, :], TQP[:, :], t_[:, 512:1024]))(), inc="cV")
                    B.op("vector", (lambda d=dsl: lambda v: v.tensor_add(
                        d, TQ[:, :], TQP[:, :]))(), inc="cV")


        fm_pass(KFM, (wk_ok, wkp_ok))
        _k_cV = B.n("cV")

        wv_ok = w_into(0, wvd, [("cP", B.n("cP"))], "wv")
        wg1f_ok = ("qW0", dma("sync",
                   WG1S.rearrange("p (kc f) -> p kc f", f=64),
                   wg1d.rearrange("(kc p) f -> p kc f", p=128), "qW0",
                   waits=[("qG", _wb["wg1"])]))
        wq_ok = None
        for nc_i in range(4):
            if nc_i == 1:
                wq_ok = w_into(1, wqd, [("cP", B.n("cP")),
                                        ("qXN", B.n("qXN"))], "wq")
            for mt in range(4):
                ch = nc_i * 4 + mt
                for half in range(2):
                    ps = PS_A if half == 0 else PS_B
                    for kc in range(8):
                        B.op("tensor",
                             (lambda p_=ps, k_=kc, m_=mt, h_=half,
                              n_=nc_i: lambda t:
                              t.matmul(p_[:, 0:512],
                                       XNTF[:, k_ * 2048 + n_ * 512 + m_ * 128:
                                            k_ * 2048 + n_ * 512 + m_ * 128 + 128],
                                       WBUF[:, k_ * 1024 + h_ * 512:
                                            k_ * 1024 + h_ * 512 + 512],
                                       start=(k_ == 0), stop=(k_ == 7)))(),
                             waits=([wv_ok,
                                     ("cV", B.n("cV"))] if kc == 0 else []),
                             inc=("cP" if kc == 7 else None))
                    B.op("vector",
                         (lambda p_=ps, c_=ch, h_=half: lambda v: v.tensor_copy(
                             VTM[:, c_ * 1024 + h_ * 512:c_ * 1024 + h_ * 512 + 512],
                             p_[:, 0:512]))(),
                         waits=[("cP", B.n("cP"))], inc="cV")
            for kc in range(8):
                B.op("tensor", (lambda k_=kc, n_=nc_i: lambda t:
                     t.matmul(PS_C[0:64, 0:512],
                              WG1S[:, k_ * 64:k_ * 64 + 64],
                              XNTF[:, k_ * 2048 + n_ * 512:
                                   k_ * 2048 + n_ * 512 + 512],
                              start=(k_ == 0), stop=(k_ == 7)))(),
                     waits=([wg1f_ok,
                             ("cV", B.n("cV"))] if kc == 0 else []),
                     inc=("cP" if kc == 7 else None))
            B.op("vector", (lambda n_=nc_i: lambda v: v.tensor_copy(
                G1FM[0:64, n_ * 512:(n_ + 1) * 512],
                PS_C[0:64, 0:512]))(),
                waits=[("cP", B.n("cP"))], inc="cV")
        _v_cV = B.n("cV")
        if dump:
            dma("scalar", attd[0:128, :].rearrange("(a p) e -> p a e", p=64),
                G1FM[0:64, :].rearrange("p (a e) -> p a e", a=2), "qA",
                waits=[("cV", B.n("cV"))])

        wqp_ok = w_into(0, wqpd, [("cP", B.n("cP"))], "wqp")
        fm_pass(QFM, (wq_ok, wqp_ok), slots=(1, 0))
        _q_cV = B.n("cV")
        wo_ok = dma("sync",
                    WOB.rearrange("p (kc f) -> p kc f", f=1024),
                    wod.rearrange("(kc p) f -> p kc f", p=128), "qW0",
                    waits=[("cP", B.n("cP")), ("cV", B.n("cV")),
                           ("qG", _wb["wo"])])
        wg2_ok = dma("scalar", WG2S[0:64, :], wg2d, "qW1",
                     waits=[("qG", _wb["wg2"])])

        for c in range(NCH):
            for ft in range(8):
                B.op("tensor", (lambda c_=c, f_=ft: lambda t: t.transpose(
                    PS_TB[:, f_ * 128:f_ * 128 + 128],
                    KFM[:, f_ * 2048 + c_ * 128:f_ * 2048 + c_ * 128 + 128],
                    IDBF[:, :]))(),
                    waits=([("cV", max(_k_cV, B.n("cV")))] if ft == 0 else []),
                    inc=("cP" if ft == 7 else None))
            B.op("vector", (lambda c_=c: lambda v: v.tensor_copy(
                KTM[:, c_ * 1024:(c_ + 1) * 1024], PS_TB[:, :]))(),
                waits=[("cP", B.n("cP"))], inc="cV")
        _ktm_cV = B.n("cV")

        B.op("vector", lambda v: v.tensor_copy(BKCB[:, :], BKC[:, :]),
             waits=[("qS", _pre_qS)], inc="cV")
        _bkcb_cV = B.n("cV")
        for kc in range(8):
            for hf in range(2):
                B.op("tensor", (lambda k_=kc, h_=hf: lambda t: t.matmul(
                    PS_F[0:1, h_ * 512:(h_ + 1) * 512],
                    BKCB[:, k_:k_ + 1],
                    WOB[:, k_ * 1024 + h_ * 512:k_ * 1024 + h_ * 512 + 512],
                    start=(k_ == 0), stop=(k_ == 7)))(),
                    waits=([("qW0", wo_ok), ("cV", _bkcb_cV)]
                           if (kc == 0 and hf == 0) else []),
                    inc=("cP" if (kc == 7 and hf == 1) else None))
        B.op("vector", lambda v: v.tensor_copy(BWOB0[0:1, :],
                                               PS_F[0:1, 0:512]),
             waits=[("cP", B.n("cP"))], inc="cV")
        B.op("vector", lambda v: v.tensor_copy(BWOB1[0:1, :],
                                               PS_F[0:1, 512:1024]),
             inc="cV")
        _bwo_cV = B.n("cV")
        for kc in range(8):
            B.op("vector", (lambda k_=kc: lambda v: v.tensor_scalar(
                WOB[:, k_ * 1024:(k_ + 1) * 1024],
                WOB[:, k_ * 1024:(k_ + 1) * 1024],
                GKC[:, k_:k_ + 1], None, ALU.mult))(),
                waits=([("cP", B.n("cP"))] if kc == 0 else []), inc="cV")
        _wof_cV = B.n("cV")

        for h in range(H):
            for c in range(NCH):
                B.op("tensor", (lambda h_=h, c_=c: lambda t: t.matmul(
                    PS_F[0:64, h_ * 64:(h_ + 1) * 64],
                    KTM[:, c_ * 1024 + h_ * 64:c_ * 1024 + h_ * 64 + 64],
                    VTM[:, c_ * 1024 + h_ * 64:c_ * 1024 + h_ * 64 + 64],
                    start=(c_ == 0), stop=(c_ == NCH - 1)))(),
                    waits=([("cV", max(_ktm_cV, _v_cV, _bwo_cV))]
                           if (h == 0 and c == 0) else []),
                    inc=("cP" if c == NCH - 1 else None))
        B.op("vector", lambda v: v.tensor_copy(SPRE[0:64, :], PS_F[0:64, :]),
             waits=[("cP", B.n("cP"))], inc="cV")
        dma("sync", sl_in.rearrange("(h dk) dv -> dk h dv", dk=64),
            SPRE[0:64, :].rearrange("p (h dv) -> p h dv", dv=64), "qSL",
            waits=[("cV", B.n("cV"))])
        B.op("gpsimd", lambda g: g.collective_compute(
            "AllGather", ALU.bypass, replica_groups=[list(range(NCORE))],
            ins=[sl_in], outs=[sl_all]),
            waits=[("qSL", B.n("qSL"))], inc="cc")
        _cc_sl = B.n("cc")


        _lg_ok = dma("sync", LG.rearrange("p (c e) -> p c e", e=8),
                     lg_all.rearrange("(c p) e -> p c e", p=128), "qS",
                     waits=[("cc", _cc_lg), ("cV", B.n("cV")),
                            ("cA", B.n("cA"))])

        def _r(fn, waits=None):
            B.op("vector", fn, waits=(waits or []), inc="cV")

        def rs(i):
            return RS[:, i * 128:(i + 1) * 128]

        def rc(i):
            return RCE[:, i * 8:(i + 1) * 8]

        def d3(buf):
            return buf.rearrange("p (c e) -> p c e", e=8)

        def bcl(col2d):
            return col2d.rearrange("p (c one) -> p c one", one=1) \
                        .to_broadcast([128, 128, 8])

        def first_max_onehot(buf, maxsl):
            b3 = d3(buf)
            R43_ = d3(R4)
            R53_ = d3(R5)
            _r(lambda v: v.reduce_max(maxsl, b3, axis=AX))
            _r(lambda v: v.tensor_tensor(b3, b3, bcl(maxsl), ALU.is_ge))
            _r(lambda v: v.memset(R4[:, :], 0.0))
            _r(lambda v: v.tensor_copy(R43_[:, :, 1:8], b3[:, :, 0:7]))
            _r(lambda v: v.memset(R5[:, :], 0.0))
            _r(lambda v: v.tensor_copy(R53_[:, :, 2:8], R43_[:, :, 0:6]))
            _r(lambda v: v.tensor_add(R4[:, :], R4[:, :], R5[:, :]))
            _r(lambda v: v.memset(R5[:, :], 0.0))
            _r(lambda v: v.tensor_copy(R53_[:, :, 4:8], R43_[:, :, 0:4]))
            _r(lambda v: v.tensor_add(R4[:, :], R4[:, :], R5[:, :]))
            _r(lambda v: v.tensor_scalar(R4[:, :], R4[:, :], 0.0, None,
                                         ALU.is_equal))
            _r(lambda v: v.tensor_mul(buf[:, :], buf[:, :], R4[:, :]))

        def cumsum_block(out_sb, tot_slot, car_slot, bnc_row):
            for hf in range(2):
                B.op("tensor", (lambda h_=hf: lambda t: t.matmul(
                    PS_E[:, h_ * 512:(h_ + 1) * 512], TRB[:, :],
                    RB16[:, h_ * 512:(h_ + 1) * 512], start=True, stop=True))(),
                    waits=([("cV", B.n("cV"))] if hf == 0 else []), inc=None)
            for hf in range(2):
                B.op("tensor", (lambda h_=hf: lambda t: t.matmul(
                    PS_F[0:1, h_ * 512:(h_ + 1) * 512], OCB[:, :],
                    RB16[:, h_ * 512:(h_ + 1) * 512], start=True, stop=True))(),
                    inc=("cP" if hf == 1 else None))
            B.op("vector", lambda v: v.tensor_copy(R4[0:1, :], PS_F[0:1, :]),
                 waits=[("cP", B.n("cP"))], inc="cV")
            dma("sync", bnc[bnc_row:bnc_row + 1, :], R4[0:1, :], "qS",
                waits=[("cV", B.n("cV"))])
            _t_ok = dma("sync", rc(tot_slot),
                        bnc[bnc_row:bnc_row + 1, :].rearrange(
                            "one (c e) -> (one c) e", e=8), "qS",
                        waits=[("qS", 0)])
            B.op("tensor", (lambda ts=tot_slot: lambda t: t.matmul(
                PS_C[:, 0:8], TRF[:, :], rc(ts), start=True, stop=True))(),
                waits=[("qS", _t_ok), ("cV", B.n("cV"))], inc="cP")
            B.op("vector", (lambda cs=car_slot: lambda v: v.tensor_copy(
                rc(cs), PS_C[:, 0:8]))(),
                waits=[("cP", B.n("cP"))], inc="cV")
            dma("sync", bnc[bnc_row + 1:bnc_row + 2, :].rearrange(
                "one (c e) -> (one c) e", e=8), rc(car_slot), "qS",
                waits=[("cV", B.n("cV"))])
            _c_ok = dma("sync", R4[0:1, :], bnc[bnc_row + 1:bnc_row + 2, :],
                        "qS", waits=[("qS", 0)])
            for hf in range(2):
                B.op("tensor", (lambda h_=hf: lambda t: t.matmul(
                    PS_F[:, h_ * 512:(h_ + 1) * 512], ORF[:, :],
                    R4[0:1, h_ * 512:(h_ + 1) * 512], start=True, stop=True))(),
                    waits=([("qS", _c_ok), ("cV", B.n("cV"))] if hf == 0 else []),
                    inc=("cP" if hf == 1 else None))
            B.op("vector", lambda v: v.tensor_copy(R4[:, :], PS_E[:, :]),
                 waits=[("cP", B.n("cP"))], inc="cV")
            B.op("vector", (lambda o=out_sb: lambda v: v.scalar_tensor_tensor(
                o[:, :], PS_F[:, :], 1.0, R4[:, :], ALU.mult, ALU.add))(),
                inc="cV")

        _r(lambda v: v.tensor_copy(R1[:, :], LG[:, :]),
           [("qS", _lg_ok), ("qSL", B.n("qSL"))])
        first_max_onehot(R1, rs(0))
        _r(lambda v: v.tensor_copy(RB16[:, :], R1[:, :]))
        cumsum_block(R5, 0, 1, 0)
        _r(lambda v: v.tensor_mul(R4[:, :], R5[:, :], R1[:, :]))
        _r(lambda v: v.reduce_sum(rs(1), d3(R4), axis=AX))
        _r(lambda v: v.tensor_scalar(rs(2), rs(1), float(CAP), None, ALU.is_lt))
        B.op("tensor", lambda t: t.matmul(PS_C[0:1, 0:8], OCF[:, :], rc(0),
                                          start=True, stop=True),
             waits=[("cV", B.n("cV"))], inc="cP")
        B.op("vector", lambda v: v.tensor_copy(RCE[0:1, 16:24], PS_C[0:1, 0:8]),
             waits=[("cP", B.n("cP"))], inc="cV")
        _r(lambda v: v.tensor_scalar(RCE[0:1, 16:24], RCE[0:1, 16:24],
                                     float(CAP), None, ALU.min))
        B.op("tensor", lambda t: t.matmul(PS_C[:, 0:8], ORF[:, :],
                                          RCE[0:1, 16:24], start=True, stop=True),
             waits=[("cV", B.n("cV"))], inc="cP")
        B.op("vector", lambda v: v.tensor_copy(rc(3), PS_C[:, 0:8]),
             waits=[("cP", B.n("cP"))], inc="cV")
        _r(lambda v: v.scalar_tensor_tensor(R2[:, :], R1[:, :], -LGW, LG[:, :],
                                            ALU.mult, ALU.add))
        first_max_onehot(R2, rs(3))
        _r(lambda v: v.tensor_scalar(rs(4), rs(2), -1.0, 1.0, ALU.mult, ALU.add))
        _r(lambda v: v.tensor_tensor(d3(R2), d3(R2), bcl(rs(4)), ALU.mult))
        _r(lambda v: v.tensor_copy(RB16[:, :], R2[:, :]))
        cumsum_block(R5, 4, 5, 2)
        _r(lambda v: v.tensor_mul(R4[:, :], R5[:, :], R2[:, :]))
        _r(lambda v: v.reduce_sum(rs(5), d3(R4), axis=AX))
        _r(lambda v: v.tensor_tensor(
            d3(R4), d3(R2),
            rc(3).rearrange("p (one e) -> p one e", one=1)
                 .to_broadcast([128, 128, 8]), ALU.mult))
        _r(lambda v: v.reduce_sum(rs(6), d3(R4), axis=AX))
        _r(lambda v: v.tensor_add(rs(7), rs(6), rs(5)))
        _r(lambda v: v.tensor_scalar(rs(7), rs(7), float(CAP), None, ALU.is_lt))
        _r(lambda v: v.tensor_mul(rs(7), rs(7), rs(4)))
        _r(lambda v: v.tensor_tensor(d3(R3), d3(R1), bcl(rs(2)), ALU.mult))
        _r(lambda v: v.tensor_tensor(d3(R4), d3(R2), bcl(rs(7)), ALU.mult))
        _r(lambda v: v.tensor_add(R3[:, :], R3[:, :], R4[:, :]))
        _r(lambda v: v.tensor_sub(rs(8), rs(4), rs(7)))
        _r(lambda v: v.tensor_copy(RB16[:, :], R3[:, :]))
        cumsum_block(R5, 6, 7, 4)
        _r(lambda v: v.tensor_mul(R4[:, :], R5[:, :], R3[:, :]))
        _r(lambda v: v.reduce_sum(rs(9), d3(R4), axis=AX))
        B.op("gpsimd", lambda g: g.iota(
            R5[:, :], pattern=[[0, 128], [CAP, 8]], base=0, channel_multiplier=0,
            allow_small_or_imprecise_dtypes=True),
            waits=[("cV", B.n("cV"))], inc="cG")
        B.op("vector", lambda v: v.tensor_mul(R4[:, :], R3[:, :], R5[:, :]),
             waits=[("cG", B.n("cG"))], inc="cV")
        _r(lambda v: v.reduce_sum(rs(10), d3(R4), axis=AX))
        _r(lambda v: v.tensor_add(rs(10), rs(10), rs(9)))
        _r(lambda v: v.tensor_copy(RB16[:, 0:128], rs(8)))
        B.op("tensor", lambda t: t.matmul(PS_E[:, 0:128], TRB[:, :],
                                          RB16[:, 0:128], start=True, stop=True),
             waits=[("cV", B.n("cV"))], inc=None)
        B.op("tensor", lambda t: t.matmul(PS_F[0:1, 0:128], OCB[:, :],
                                          RB16[:, 0:128], start=True, stop=True),
             inc="cP")
        B.op("vector", lambda v: v.tensor_copy(R4[0:1, 0:128], PS_F[0:1, 0:128]),
             waits=[("cP", B.n("cP"))], inc="cV")
        dma("sync", bnc[6:7, 0:128], R4[0:1, 0:128], "qS",
            waits=[("cV", B.n("cV"))])
        _f_ok = dma("sync", RCE[:, 24:25], bnc[6:7, 0:128], "qS",
                    waits=[("qS", 0)])
        B.op("tensor", lambda t: t.matmul(PS_C[:, 0:1], TRF[:, :], RCE[:, 24:25],
                                          start=True, stop=True),
             waits=[("qS", _f_ok), ("cV", B.n("cV"))], inc="cP")
        B.op("vector", lambda v: v.tensor_copy(RCE[:, 25:26], PS_C[:, 0:1]),
             waits=[("cP", B.n("cP"))], inc="cV")
        dma("sync", bnc[7:8, 0:128], RCE[:, 25:26], "qS",
            waits=[("cV", B.n("cV"))])
        _f2_ok = dma("sync", R4[0:1, 0:128], bnc[7:8, 0:128], "qS",
                     waits=[("qS", 0)])
        B.op("tensor", lambda t: t.matmul(PS_F[:, 0:128], ORF[:, :],
                                          R4[0:1, 0:128], start=True, stop=True),
             waits=[("qS", _f2_ok), ("cV", B.n("cV"))], inc="cP")
        B.op("vector", lambda v: v.tensor_copy(rs(11), PS_E[:, 0:128]),
             waits=[("cP", B.n("cP"))], inc="cV")
        _r(lambda v: v.scalar_tensor_tensor(rs(11), PS_F[:, 0:128], 1.0, rs(11),
                                            ALU.mult, ALU.add))
        _r(lambda v: v.reduce_sum(rs(0), d3(R3), axis=AX))
        _r(lambda v: v.tensor_scalar(rs(11), rs(11), float(NT), None, ALU.add))
        _r(lambda v: v.tensor_mul(rs(10), rs(10), rs(0)))
        _r(lambda v: v.tensor_scalar(rs(1), rs(0), -1.0, 1.0, ALU.mult, ALU.add))
        _r(lambda v: v.tensor_mul(rs(11), rs(11), rs(1)))
        _r(lambda v: v.tensor_add(rs(10), rs(10), rs(11)))
        B.op("gpsimd", lambda g: g.iota(
            R5[:, :], pattern=[[0, 128], [FBPC, 8]], base=0,
            channel_multiplier=0, allow_small_or_imprecise_dtypes=True),
            waits=[("cV", B.n("cV"))], inc="cG")
        B.op("vector", lambda v: v.tensor_mul(R4[:, :], R3[:, :], R5[:, :]),
             waits=[("cG", B.n("cG"))], inc="cV")
        _r(lambda v: v.reduce_sum(rs(3), d3(R4), axis=AX))
        _r(lambda v: v.tensor_add(rs(3), rs(3), rs(9)))
        _r(lambda v: v.tensor_scalar(rs(5), rs(9), float(FBPC), None, ALU.is_ge))
        _r(lambda v: v.tensor_scalar(rs(5), rs(5), float(8192 - FBPC), None,
                                     ALU.mult))
        _r(lambda v: v.tensor_add(rs(3), rs(3), rs(5)))
        _r(lambda v: v.tensor_scalar(rs(6), rs(10), float(NT), None,
                                     ALU.subtract))
        _r(lambda v: v.tensor_scalar(rs(6), rs(6), 0.0, None, ALU.max))
        _r(lambda v: v.tensor_scalar(rs(6), rs(6), float(2 * 8192), None,
                                     ALU.add))
        _r(lambda v: v.tensor_scalar(rs(7), rs(10), float(NT), None, ALU.is_lt))
        _r(lambda v: v.tensor_mul(rs(3), rs(3), rs(7)))
        _r(lambda v: v.tensor_scalar(rs(8), rs(7), -1.0, 1.0, ALU.mult, ALU.add))
        _r(lambda v: v.tensor_mul(rs(6), rs(6), rs(8)))
        _r(lambda v: v.tensor_add(rs(3), rs(3), rs(6)))
        B.op("tensor", lambda t: t.transpose(PS_B[:, 0:128], rs(10), IDF[:, :]),
             waits=[("cV", B.n("cV"))], inc="cP")
        B.op("vector", lambda v: v.tensor_copy(rs(11), PS_B[:, 0:128]),
             waits=[("cP", B.n("cP"))], inc="cV")
        _r(lambda v: v.tensor_copy(IDXI[:, :], rs(11)))
        _ok1 = dma("sync", idxall.rearrange("(c p) one -> c p one", p=128),
                   IDXI.rearrange("p (c one) -> p c one", one=1), "qJ",
                   waits=[("cV", B.n("cV"))])
        B.op("tensor", lambda t: t.transpose(PS_B[:, 0:128], rs(3), IDF[:, :]),
             waits=[("cV", B.n("cV"))], inc="cP")
        B.op("vector", lambda v: v.tensor_copy(rs(11), PS_B[:, 0:128]),
             waits=[("cP", B.n("cP")), ("qJ", _ok1)], inc="cV")
        _ph3_ps_cV = B.n("cV")
        _r(lambda v: v.tensor_copy(IDXI[:, :], rs(11)))
        _ok2 = dma("sync", idxy.rearrange("(c p) one -> c p one", p=128),
                   IDXI.rearrange("p (c one) -> p c one", one=1), "qJ",
                   waits=[("cV", B.n("cV"))])

        B.op("gpsimd", lambda g: g.indirect_dma_start(
            out=OWNW[:, :], out_offset=None, in_=idxall[:, :],
            in_offset=IndirectOffsetOnAxis(IIX[:, 0:1], 0)),
            waits=[("qJ", _ok1)], inc="qI", inc_by=16)
        B.op("gpsimd", lambda g: g.indirect_dma_start(
            out=OWNY[:, :], out_offset=None, in_=idxy[:, :],
            in_offset=IndirectOffsetOnAxis(IIX[:, 0:1], 0)),
            waits=[("qJ", _ok2)], inc="qI", inc_by=16)
        _own_qI = B.n("qI")
        for j in range(16):
            B.op("gpsimd", (lambda j_=j: lambda g: g.indirect_dma_start(
                out=tbl_in[:, :],
                out_offset=IndirectOffsetOnAxis(OWNW[:, j_:j_ + 1], 0),
                in_=TIDS[:, j_:j_ + 1], in_offset=None,
                bounds_check=bc_regs[TBLN - 1], oob_is_err=False))(),
                waits=([("qI", _own_qI), ("qS", _pre_qS)] if j == 0 else []),
                inc="qI", inc_by=16)
        B.op("gpsimd", lambda g: g.collective_compute(
            "AllReduce", ALU.add, replica_groups=[list(range(NCORE))],
            ins=[tbl_in], outs=[tbl_all]),
            waits=[("qI", B.n("qI"))], inc="cc")
        _cc_tbl = B.n("cc")
        for col in range(24):
            B.op("gpsimd", (lambda c_=col: lambda g: g.indirect_dma_start(
                out=TBLW[:, c_:c_ + 1], out_offset=None, in_=tbl_all[:, :],
                in_offset=IndirectOffsetOnAxis(IIX2[:, c_:c_ + 1], 0)))(),
                waits=([("cc", _cc_tbl)] if col == 0 else []),
                inc="qI", inc_by=16)
        B.op("vector", lambda v: v.tensor_copy(TBLF[:, :], TBLW[:, :]),
             waits=[("qI", B.n("qI"))], inc="cV")
        B.op("vector", lambda v: v.tensor_scalar(
            TBLF[:, :], TBLF[:, :], 1.0, None, ALU.subtract), inc="cV")
        B.op("vector", lambda v: v.tensor_scalar(
            SMALL[:, 24:48], TBLF[:, :], 0.0, None, ALU.is_lt), inc="cV")
        B.op("vector", lambda v: v.scalar_tensor_tensor(
            TBLF[:, :], SMALL[:, 24:48], 9.9e5, TBLF[:, :], ALU.mult, ALU.add),
            inc="cV")
        B.op("vector", lambda v: v.tensor_copy(TBLW[:, :], TBLF[:, :]), inc="cV")
        _tblw_cV = B.n("cV")


        B.op("vector", lambda v: v.memset(SPREF[:, :], 0.0),
             waits=[("cc", _cc_sl)], inc="cV")
        for r in range(NCORE):
            src = sl_all[r * 1024:(r + 1) * 1024, :].rearrange(
                "(h dk) dv -> dk h dv", dk=64)
            _sl_ok = dma("sync",
                         XLP[0:64, :].rearrange("p (h dv) -> p h dv", dv=64),
                         src, "qS",
                         waits=[("cc", _cc_sl), ("cV", B.n("cV"))])
            _sl_ok = dma("sync",
                         XLP[64:128, :].rearrange("p (h dv) -> p h dv", dv=64),
                         src, "qS")
            B.op("vector", (lambda r_=r: lambda v: v.scalar_tensor_tensor(
                SPREF[:, :], XLP[:, :], PMB[:, r_:r_ + 1],
                SPREF[:, :], ALU.mult, ALU.add))(),
                waits=[("qS", _sl_ok)], inc="cV")
        B.op("vector", lambda v: v.tensor_copy(SOP[:, :], SPREF[:, :]),
             inc="cV")
        B.op("vector", lambda v: v.memset(SST[:, :], 0.0), inc="cV")
        _sop_cV = B.n("cV")
        _sst_cV = B.n("cV")

        spark_tiles = spark.rearrange("(c p) e -> c p e", p=128)
        etm_sl = lambda h: ETM[:, (h % 4) * 128:(h % 4) * 128 + 128]
        ps_free = {0: _ph3_ps_cV, 1: _ph3_ps_cV}
        etm_free = {i: 0 for i in range(4)}
        head_state = []

        def plan_et(c, h):
            par = h % 2
            ps_et = PS_A if par == 0 else PS_B
            kf = KFM[(h % 2) * 64:(h % 2) * 64 + 64,
                     (h // 2) * 2048 + c * 128:(h // 2) * 2048 + c * 128 + 128]
            qf = QFM[(h % 2) * 64:(h % 2) * 64 + 64,
                     (h // 2) * 2048 + c * 128:(h // 2) * 2048 + c * 128 + 128]
            B.op("tensor", (lambda p_=ps_et, kf_=kf, qf_=qf: lambda t:
                 t.matmul(p_[:, 0:128], kf_, qf_, start=True, stop=True))(),
                waits=[("cV", max(_q_cV, ps_free[par]))], inc="cP")
            B.op("vector", (lambda p_=ps_et, h_=h: lambda v: v.tensor_tensor(
                etm_sl(h_), p_[:, 0:128], MSK[:, :], ALU.mult))(),
                waits=[("cP", B.n("cP")), ("cP", etm_free[h % 4])], inc="cV")
            ps_free[par] = B.n("cV")
            head_state.append((c, h, qf, B.n("cV")))

        def plan_consume():
            c, h, qf, mask_cV = head_state.pop(0)
            B.op("tensor", (lambda c_=c, h_=h: lambda t: t.matmul(
                PS_E[:, h_ * 64:(h_ + 1) * 64], etm_sl(h_),
                VTM[:, c_ * 1024 + h_ * 64:c_ * 1024 + h_ * 64 + 64],
                start=True, stop=False))(),
                waits=[("cV", mask_cV)] +
                      ([("cV", ph4_og.get(c - 1, 0))] if h == 0 else []),
                inc=None)
            B.op("tensor", (lambda h_=h, qf_=qf, i_=h % 2: lambda t: t.matmul(
                PS_E[:, h_ * 64:(h_ + 1) * 64], qf_,
                SOP[i_ * 64:i_ * 64 + 64, h_ * 64:(h_ + 1) * 64],
                start=False, stop=True))(),
                waits=([("cV", ph4_sop[c])] if h == 0 else []), inc=None)
            B.op("tensor", (lambda c_=c, h_=h: lambda t: t.matmul(
                PS_F[0:64, h_ * 64:(h_ + 1) * 64],
                KTM[:, c_ * 1024 + h_ * 64:c_ * 1024 + h_ * 64 + 64],
                VTM[:, c_ * 1024 + h_ * 64:c_ * 1024 + h_ * 64 + 64],
                start=True, stop=True))(),
                waits=([("cV", ph4_sst[c])] if h == 0 else []), inc=None)
            B.op("tensor", (lambda c_=c, h_=h: lambda t: t.matmul(
                PS_F[64:128, h_ * 64:(h_ + 1) * 64],
                KTM[:, c_ * 1024 + h_ * 64:c_ * 1024 + h_ * 64 + 64],
                VTM[:, c_ * 1024 + h_ * 64:c_ * 1024 + h_ * 64 + 64],
                start=True, stop=True, tile_position=(0, 64)))(),
                inc="cP")
            etm_free[h % 4] = B.n("cP")

        ph4_sop = {0: _sop_cV}
        ph4_sst = {0: _sst_cV}
        ph4_og = {}
        epi_sig_cA = {}
        epi_sout_cV = {}

        epi_sts_cV = {}

        def plan_epi_A(c):
            spb = SPB0 if c % 2 == 0 else SPB1
            _spb_cV = epi_spb[c]
            for j in range(8):
                B.op("tensor", (lambda j_=j, s_=spb: lambda t: t.transpose(
                    PS_TB[:, j_ * 128:j_ * 128 + 128],
                    s_[:, j_ * 128:j_ * 128 + 128], IDBF[:, :]))(),
                    waits=([("cV", _spb_cV)] if j == 0 else []),
                    inc=("cP" if j == 7 else None))
            B.op("vector", lambda v: v.tensor_copy(STS[:, :], PS_TB[:, :]),
                 waits=[("cP", B.n("cP"))], inc="cV")
            epi_sts_cV[c] = B.n("cV")

        def plan_epi_B(c):
            _psc_cA = epi_sig_cA.get(c + 1, B.n("cA"))
            _psc_cV = max(_wof_cV, epi_sts_cV[c], epi_sout_cV.get(c - 1, 0))
            so_sl = (c % 2) * 1024
            for half in range(2):
                for j in range(8):
                    B.op("tensor", (lambda j_=j, h_=half: lambda t:
                         t.matmul(PS_C[:, 0:512], STS[:, j_ * 128:j_ * 128 + 128],
                                  WOB[:, j_ * 1024 + h_ * 512:
                                       j_ * 1024 + h_ * 512 + 512],
                                  start=(j_ == 0), stop=False))(),
                         waits=([("cA", _psc_cA), ("cV", _psc_cV)]
                                if j == 0 else []),
                         inc=None)
                B.op("tensor", (lambda h_=half: lambda t: t.matmul(
                    PS_C[:, 0:512], ORB[0:1, 0:128],
                    (BWOB0 if h_ == 0 else BWOB1)[0:1, :],
                    start=False, stop=True))(),
                    inc="cP")
                B.op("vector", (lambda h_=half, s_=so_sl: lambda v:
                     v.tensor_copy(SOUT[:, s_ + h_ * 512:s_ + h_ * 512 + 512],
                                   PS_C[:, 0:512]))(),
                     waits=[("cP", B.n("cP")), ("qA", B.n("qA"))], inc="cV")
                _psc_cV = B.n("cV")
            epi_sout_cV[c] = B.n("cV")
            dma("scalar", spark_tiles[c, :, :], SOUT[:, so_sl:so_sl + 1024],
                "qA", waits=[("cV", B.n("cV"))])

        epi_spb = {}
        heads_cP_hist = {}
        for c in range(NCH):
            if c >= 1:
                plan_epi_A(c - 1)
            gt = GATE if c % 2 == 0 else GATE1
            for half in range(2):
                gw = ([("cV", B.n("cV")), ("cA", B.n("cA"))] if c < 2 else
                      [("cV", epi_sout_cV[c - 2])])
                B.op("tensor", (lambda c_=c, h_=half: lambda t: t.matmul(
                    PS_C[:, 0:512], G1FM[0:64, c_ * 128:c_ * 128 + 128],
                    WG2S[0:64, h_ * 512:(h_ + 1) * 512],
                    start=True, stop=True))(),
                    waits=(gw + [("qW1", wg2_ok)])
                        if half == 0 else [("cA", B.n("cA"))],
                    inc="cP")
                B.op("scalar", (lambda g_=gt, h_=half: lambda s: s.activation(
                    g_[:, h_ * 512:(h_ + 1) * 512], PS_C[:, 0:512],
                    AF.Sigmoid))(),
                    waits=[("cP", B.n("cP"))], inc="cA")
            epi_sig_cA[c] = B.n("cA")
            for h in range(H):
                plan_et(c, h)
                if len(head_state) >= 2:
                    plan_consume()
            while head_state:
                plan_consume()
            _heads_cP = B.n("cP")
            heads_cP_hist[c] = _heads_cP
            if c % 4 == 3:
                for k2 in (c // 4, c // 4 + 4):
                    dma("gpsimd", W1B[:, k2 * 4096:(k2 + 1) * 4096],
                        w1d[k2 * 128:(k2 + 1) * 128, :], "qZ0",
                        waits=[("cP", _heads_cP), ("qG", _wb["w1m"])])
            if c < NCH - 1:
                B.op("vector", lambda v: v.scalar_tensor_tensor(
                    SST[:, :], PS_F[:, :], 1.0, SST[:, :],
                    ALU.mult, ALU.add),
                    waits=[("cP", _heads_cP)], inc="cV")
                ph4_sst[c + 1] = B.n("cV")
                B.op("vector", lambda v: v.tensor_add(
                    SOP[:, :], SST[:, :], SPREF[:, :]), inc="cV")
                ph4_sop[c + 1] = B.n("cV")
            B.op("vector", (lambda g_=gt: lambda v: v.tensor_tensor(
                OG[:, :], PS_E[:, :], g_[:, :], ALU.mult))(),
                 waits=[("cP", _heads_cP), ("cA", B.n("cA"))], inc="cV")
            ph4_og[c] = B.n("cV")
            if dump:
                dma("scalar", ogd.rearrange("(c p) e -> c p e", p=128)[c, :, :],
                    gt[:, :], "qA", waits=[("cV", B.n("cV"))])
            B.op("vector", lambda v: v.reduce_sum(SMALL[:, 8:9], OG[:, :],
                                                  axis=AX), inc="cV")
            B.op("scalar", lambda s: s.activation(SPRE[:, :], OG[:, :],
                                                  AF.Square,
                                                  accum_out=SMALL[:, 9:10]),
                 waits=[("cV", B.n("cV")), ("qA", B.n("qA"))], inc="cA")
            B.op("vector", lambda v: v.tensor_scalar(
                SMALL[:, 10:11], SMALL[:, 8:9], 1.0 / E, None, ALU.mult),
                waits=[("cA", B.n("cA"))], inc="cV")
            B.op("vector", lambda v: v.tensor_mul(
                SMALL[:, 11:12], SMALL[:, 10:11], SMALL[:, 10:11]), inc="cV")
            B.op("vector", lambda v: v.tensor_scalar(
                SMALL[:, 12:13], SMALL[:, 9:10], 1.0 / E, None, ALU.mult),
                inc="cV")
            B.op("vector", lambda v: v.tensor_sub(
                SMALL[:, 12:13], SMALL[:, 12:13], SMALL[:, 11:12]), inc="cV")
            B.op("scalar", lambda s: s.activation(
                SMALL[:, 13:14], SMALL[:, 12:13], AF.Ln, bias=1e-5, scale=1.0),
                waits=[("cV", B.n("cV"))], inc="cA")
            B.op("scalar", lambda s: s.activation(
                SMALL[:, 14:15], SMALL[:, 13:14], AF.Exp, bias=0.0, scale=-0.5),
                inc="cA")
            B.op("vector", lambda v: v.tensor_scalar(
                SPRE[:, :], OG[:, :], SMALL[:, 10:11], SMALL[:, 14:15],
                ALU.subtract, ALU.mult),
                waits=[("cA", B.n("cA"))], inc="cV")
            spb = SPB0 if c % 2 == 0 else SPB1
            B.op("vector", (lambda s_=spb: lambda v: v.tensor_copy(
                s_[:, :], SPRE[:, :]))(),
                waits=[("cP", B.n("cP"))], inc="cV")
            epi_spb[c] = B.n("cV")
            if c >= 1:
                plan_epi_B(c - 1)
        plan_epi_A(NCH - 1)
        plan_epi_B(NCH - 1)
        _ph4_cP = B.n("cP")
        _ph4_cV = B.n("cV")

        _b2_ok = dma("sync", B2ROW[:, :], b2x, "qS",
                     waits=[("cV", _ph4_cV), ("cA", B.n("cA")),
                            ("qA", B.n("qA"))])
        for h0 in range(4):
            B.op("tensor", (lambda h_=h0: lambda t: t.matmul(
                PS_A[:, 0:512], ORF[:, :],
                B2ROW[0:1, h_ * 512:(h_ + 1) * 512], start=True, stop=True))(),
                waits=[("qS", _b2_ok), ("cV", B.n("cV")), ("cP", _ph4_cP)],
                inc="cP")
            B.op("vector", (lambda h_=h0: lambda v: v.tensor_copy(
                B2BC[:, h_ * 512:(h_ + 1) * 512], PS_A[:, 0:512]))(),
                waits=[("cP", B.n("cP"))], inc="cV")
        _b2bc_cV = B.n("cV")

        _h15 = heads_cP_hist[NCH - 1]
        for hf in range(2):
            dma("gpsimd",
                W2B[:, hf * 16384:(hf + 1) * 16384].rearrange(
                    "p (fc e) -> p fc e", e=1024),
                w2d[hf * 2048:(hf + 1) * 2048, :].rearrange(
                    "(fc p) e -> p fc e", p=128), "qZ1",
                waits=[("cP", _h15), ("qG", _wb["w2m"])])
        _w2m_qX1 = B.n("qZ1")
        _w1m_qX0 = B.n("qZ0")
        XG3 = XG.rearrange("p (kc n) -> p kc n", n=512)
        PT3 = PS_TB.rearrange("p (kc n) -> p kc n", n=128)
        nblocks = [(nb, "e") for nb in range(4)] + [(nb, "f") for nb in range(2)]

        def plan_gather_dma(bi):
            last = 0
            for m in range(4):
                col = bi * 4 + m
                xgr_sl = m * 1024
                B.op("gpsimd", (lambda s_=xgr_sl: lambda g: g.memset(
                    XGR[:, s_:s_ + 1024], 0.0))(),
                    waits=[("cV", _tblw_cV), ("cP", B.n("cP")),
                           ("qI", B.n("qI")), ("qA", B.n("qA"))], inc="cG")
                B.op("gpsimd", (lambda s_=xgr_sl, c_=col: lambda g:
                     g.indirect_dma_start(
                         out=XGR[:, s_:s_ + 1024], out_offset=None,
                         in_=xnall[:, :],
                         in_offset=IndirectOffsetOnAxis(TBLW[:, c_:c_ + 1], 0),
                         bounds_check=bc_regs[NT - 1], oob_is_err=False))(),
                    waits=[("cG", B.n("cG")), ("cc", _cc_xn)],
                    inc="qI", inc_by=16)
                last = B.n("qI")
            return last

        def plan_gather_pe(bi, qig):
            for m in range(4):
                xgr_sl = m * 1024
                for kc in range(8):
                    B.op("tensor", (lambda s_=xgr_sl, k_=kc: lambda t: t.transpose(
                        PT3[:, k_, :], XGR[:, s_ + k_ * 128:s_ + k_ * 128 + 128],
                        IDBF[:, :]))(),
                        waits=(([("qI", qig)] if m == 0 else []) +
                               [("cV", B.n("cV"))] if kc == 0 else []),
                        inc=("cP" if kc == 7 else None))
                B.op("vector", (lambda m_=m: lambda v: v.tensor_copy(
                    XG3[:, :, m_ * 128:m_ * 128 + 128], PT3[:, :, :]))(),
                    waits=[("cP", B.n("cP"))], inc="cV")
            return B.n("cV")

        _g0 = plan_gather_dma(0)
        xg_ready = {0: plan_gather_pe(0, _g0)}
        sw_ok = {}
        PSY = (PS_A, PS_B, PS_E, PS_F)
        _cc_y = None
        yev01 = {}
        for bi, (nb, kind) in enumerate(nblocks):
            _xg_cV = xg_ready[bi]
            _w1_free = _b2bc_cV if bi == 0 else yev01[bi - 1]
            bcol = 0 if kind == "e" else 32
            w1w = ([("qZ0", _w1m_qX0)] if bi == 0 else
                   ([("qZ0", sw_ok["w1"])] if (kind == "f" and nb == 0) else []))
            gelu_hist = {}
            for ft in range(32):
                ps_h = PS_C if ft % 2 == 0 else PS_A
                w_g = gelu_hist.get(ft - 2, B.n("cA") if ft < 2 else 0)
                for kc in range(8):
                    B.op("tensor", (lambda p_=ps_h, f_=ft, k_=kc:
                         lambda t: t.matmul(
                             p_[:, 0:512],
                             W1B[:, k_ * 4096 + f_ * 128:
                                  k_ * 4096 + f_ * 128 + 128],
                             XG[:, k_ * 512:(k_ + 1) * 512],
                             start=(k_ == 0), stop=(k_ == 7)))(),
                         waits=([("cV", max(_xg_cV, _w1_free)),
                                 ("cA", w_g)] +
                                (w1w if ft == 0 else [])
                                if kc == 0 else []),
                         inc=("cP" if kc == 7 else None))
                B.op("scalar", (lambda p_=ps_h, f_=ft, b_=bcol: lambda s:
                     s.activation(
                         HSB[:, f_ * 512:(f_ + 1) * 512], p_[:, 0:512], AF.Gelu,
                         bias=B1SS[:, b_ + f_:b_ + f_ + 1]))(),
                    waits=[("cP", B.n("cP"))], inc="cA")
                gelu_hist[ft] = B.n("cA")
            _h_cA = B.n("cA")
            if bi == 3:
                for k2 in range(8):
                    dma("gpsimd", W1B[:, k2 * 4096:(k2 + 1) * 4096],
                        sw1d[k2 * 128:(k2 + 1) * 128, :], "qZ0",
                        waits=([("cP", B.n("cP"))] if k2 == 0 else []) +
                              [("qG", _wb["sw1"])])
                sw_ok["w1"] = B.n("qZ0")
            b2c = 0 if kind == "e" else E
            qy = "qYA" if (kind == "e" and nb < 2) else (
                "qYB" if kind == "e" else "qYC")
            w2w = ([("qZ1", _w2m_qX1)] if bi == 0 else
                   ([("qZ1", sw_ok["w2"])] if (kind == "f" and nb == 0) else []))
            for half in range(2):
                for fc in range(32):
                    for mt in range(4):
                        B.op("tensor", (lambda p_=PSY[mt], f_=fc, m_=mt,
                                        h_=half: lambda t: t.matmul(
                            p_[:, 0:512],
                            HSB[:, f_ * 512 + m_ * 128:f_ * 512 + m_ * 128 + 128],
                            W2B[:, f_ * 1024 + h_ * 512:
                                 f_ * 1024 + h_ * 512 + 512],
                            start=(f_ == 0), stop=(f_ == 31)))(),
                            waits=([("cA", _h_cA), ("cV", B.n("cV"))] + w2w
                                   if (fc == 0 and mt == 0) else []),
                            inc=("cP" if (fc % 8 == 7 and mt == 3) else None))
                    if half == 1 and bi + 1 < len(nblocks):
                        if fc == 7:
                            gq = plan_gather_dma(bi + 1)
                        elif fc == 23:
                            xg_ready[bi + 1] = plan_gather_pe(bi + 1, gq)
                _y_cP = B.n("cP")
                for mt in range(4):
                    yev_sl = (mt % 2) * 1024
                    B.op("vector", (lambda p_=PSY[mt], y_=yev_sl, h_=half,
                                    b_=b2c: lambda v: v.scalar_tensor_tensor(
                        YEV[:, y_ + h_ * 512:y_ + h_ * 512 + 512],
                        p_[:, 0:512], 1.0,
                        B2BC[:, b_ + h_ * 512:b_ + h_ * 512 + 512],
                        ALU.mult, ALU.add))(),
                        waits=[("cP", _y_cP), (qy, B.n(qy)),
                               ("cV", max(_b2bc_cV, B.n("cV")))], inc="cV")
                    if half == 1 and mt == 0:
                        yev01[bi] = B.n("cV")
                    base = (0 if kind == "e" else CAP) + nb * 512 + mt * 128
                    dma("gpsimd",
                        ymine[base:base + 128, half * 512:(half + 1) * 512],
                        YEV[:, yev_sl + half * 512:yev_sl + half * 512 + 512],
                        qy, waits=[("cV", B.n("cV"))])
            if bi == 3:
                for hf in range(2):
                    dma("gpsimd",
                        W2B[:, hf * 16384:(hf + 1) * 16384].rearrange(
                            "p (fc e) -> p fc e", e=1024),
                        sw2d[hf * 2048:(hf + 1) * 2048, :].rearrange(
                            "(fc p) e -> p fc e", p=128), "qZ1",
                        waits=([("cP", B.n("cP"))] if hf == 0 else []) +
                              [("qG", _wb["sw2"])])
                sw_ok["w2"] = B.n("qZ1")
            if bi == 1:
                B.op("gpsimd", lambda g: g.collective_compute(
                    "AllGather", ALU.bypass,
                    replica_groups=[list(range(NCORE))],
                    ins=[ymine[0:1024, :]], outs=[yall[0:8192, :]]),
                    waits=[("qYA", B.n("qYA"))], inc="cc")
            if bi == 3:
                B.op("gpsimd", lambda g: g.collective_compute(
                    "AllGather", ALU.bypass,
                    replica_groups=[list(range(NCORE))],
                    ins=[ymine[1024:2048, :]], outs=[yall[8192:16384, :]]),
                    waits=[("qYB", B.n("qYB"))], inc="cc")
            if bi == 5:
                B.op("gpsimd", lambda g: g.collective_compute(
                    "AllGather", ALU.bypass,
                    replica_groups=[list(range(NCORE))],
                    ins=[ymine[2048:3072, :]], outs=[yall[16384:24576, :]]),
                    waits=[("qYC", B.n("qYC"))], inc="cc")
        _cc_y = B.n("cc")

        x_w = x_in.rearrange("(p w) e -> w p e", w=16)
        s_w = spark.rearrange("(p w) e -> w p e", w=16)
        o_w = out_t.rearrange("(p w) e -> w p e", w=16)
        _ph5_cV = B.n("cV")
        ph6_v = {}
        ph6_o = {}
        for j in range(16):
            sl2 = slice((j % 2) * 1024, (j % 2) * 1024 + 1024)
            if j >= 2:
                wgate = [("cV", ph6_v[j - 2]), ("qO", ph6_o[j - 2])]
            else:
                wgate = [("cV", _ph5_cV)]
            B.op("gpsimd", (lambda s_=sl2: lambda g: g.memset(YGB[:, s_], 0.0))(),
                 waits=wgate + [("cc", _cc_y), ("cA", B.n("cA"))], inc="cG")
            B.op("gpsimd", (lambda j_=j, s_=sl2: lambda g: g.indirect_dma_start(
                out=YGB[:, s_], out_offset=None, in_=yall[:, :],
                in_offset=IndirectOffsetOnAxis(OWNY[:, j_:j_ + 1], 0),
                bounds_check=bc_regs[NCORE * ROWS - 1], oob_is_err=False))(),
                waits=[("cG", B.n("cG"))], inc="qI", inc_by=16)
            _yg_qI = B.n("qI")
            qx6 = "qX0" if j % 2 == 0 else "qX1"
            qd6 = "qD0" if j % 2 == 0 else "qD1"
            _x_ok = dma("sync", XLD[:, sl2], x_w[j], qx6, waits=wgate)
            _s_ok = dma("scalar", SLD[:, sl2], s_w[j], qd6,
                        waits=wgate + [("qA", B.n("qA"))])
            B.op("scalar", (lambda s_=sl2: lambda s: s.activation(
                OUTT[:, s_], YGB[:, s_], AF.Silu))(),
                waits=[("qI", _yg_qI), ("cA", B.n("cA"))], inc="cA")
            B.op("vector", (lambda s_=sl2: lambda v: v.tensor_mul(
                OUTT[:, s_], OUTT[:, s_], SLD[:, s_]))(),
                waits=[("cA", B.n("cA")), (qd6, _s_ok)], inc="cV")
            B.op("vector", (lambda s_=sl2: lambda v: v.tensor_add(
                OUTT[:, s_], OUTT[:, s_], XLD[:, s_]))(),
                waits=[(qx6, _x_ok)], inc="cV")
            ph6_v[j] = B.n("cV")
            B.op("sync", (lambda j_=j, s_=sl2: lambda e: e.dma_start(
                out=o_w[j_], in_=OUTT[:, s_]))(),
                waits=[("cV", B.n("cV"))], inc="qO", inc_by=16)
            ph6_o[j] = B.n("qO")

        for nm, src in (("d_xnall", xnall), ("d_lgall", lg_all),
                        ("d_slall", sl_all), ("d_tbl", tbl_all),
                        ("d_yall", yall)):
            if nm in dbgd:
                B.op("sync", (lambda d=dbgd[nm], s=src: lambda e: e.dma_start(
                    out=d, in_=s))(),
                    waits=[("cc", B.n("cc")), ("qO", B.n("qO"))],
                    inc="qO", inc_by=16)

        B.op("sync", lambda e: e.nop(), waits=[("qO", B.n("qO"))], inc=None)

        with nc.Block() as block:
            def mk(name):
                def body(eng):
                    if name == "gpsimd":
                        for v_ in (NT - 1, TBLN - 1, NCORE * ROWS - 1):
                            if v_ not in bc_regs:
                                bc_regs[v_] = eng.to_reg(v_)
                    for waits, fn, inc, inc_by in B.q[name]:
                        for s_, v_ in waits:
                            if v_ > 0:
                                eng.wait_ge(semh[s_], v_)
                        inst = fn(eng)
                        if inc is not None:
                            inst.then_inc(semh[inc], inc_by)
                return body
            block.sync(mk("sync"))
            block.scalar(mk("scalar"))
            block.vector(mk("vector"))
            block.tensor(mk("tensor"))
            block.gpsimd(mk("gpsimd"))

    return nc


def _host_inputs(inputs):
    x = np.ascontiguousarray(
        np.asarray(inputs["L0_attn_x"]).reshape(NT, E)).astype(np.float32)
    Wq = np.ascontiguousarray(np.asarray(inputs["L0_attn_Wq"], dtype=np.float32))
    Wk = np.ascontiguousarray(np.asarray(inputs["L0_attn_Wk"], dtype=np.float32))
    Wv = np.ascontiguousarray(np.asarray(inputs["L0_attn_Wv"], dtype=np.float32))
    Wo = np.ascontiguousarray(np.asarray(inputs["L0_attn_Wo"], dtype=np.float32))
    perm = np.zeros(E, np.int64)
    for h in range(H):
        perm[h * 64:h * 64 + 32] = np.arange(h * 64 + 32, h * 64 + 64)
        perm[h * 64 + 32:h * 64 + 64] = np.arange(h * 64, h * 64 + 32)
    Wqp = np.ascontiguousarray(Wq[:, perm])
    Wkp = np.ascontiguousarray(Wk[:, perm])
    inv_freq = (10000.0 ** (-(np.arange(D // 2, dtype=np.float64) * 2.0) / D)
                ).astype(np.float32)
    gam = np.asarray(inputs["L0_attn_gamma"], dtype=np.float32)
    bet = np.asarray(inputs["L0_attn_beta"], dtype=np.float32)
    gkc = np.ascontiguousarray(gam.reshape(8, 128).T)
    bkc = np.ascontiguousarray(bet.reshape(8, 128).T)
    in_maps = []
    p = np.arange(128, dtype=np.int32)
    for core in range(NCORE):
        t0 = core * T
        pos0 = t0 % 8192
        pos = np.arange(pos0, pos0 + T, dtype=np.float32)
        ang = (pos[:, None] * inv_freq[None, :]).astype(np.float32)
        csin = np.ascontiguousarray(np.concatenate(
            [np.cos(ang).astype(np.float32).T,
             -np.sin(ang).astype(np.float32).T,
             np.sin(ang).astype(np.float32).T], axis=0))
        pm = np.zeros((1, 8), np.float32)
        for r in range((core // 4) * 4, core):
            pm[0, r] = 1.0
        iix = np.zeros((128, 8), np.int32)
        iix[:, 0] = t0 + p * 16
        iix2 = np.zeros((128, 24), np.int32)
        for bi in range(4):
            for m in range(4):
                iix2[:, bi * 4 + m] = core * CAP + bi * 512 + m * 128 + p
        for bi in range(2):
            for m in range(4):
                iix2[:, 16 + bi * 4 + m] = (NT + core * FBPC + bi * 512 +
                                            m * 128 + p)
        tid = np.zeros((128, 16), np.int32)
        for j in range(16):
            tid[:, j] = t0 + p * 16 + j + 1
        b1v = np.asarray(inputs["L0_ffn_b1"], dtype=np.float32)[core]
        sb1v = np.asarray(inputs["L0_ffn_sb1"], dtype=np.float32)
        b1mat = np.ascontiguousarray(np.concatenate(
            [b1v.reshape(32, 128).T, sb1v.reshape(32, 128).T], axis=1))
        b2row = np.ascontiguousarray(np.concatenate(
            [np.asarray(inputs["L0_ffn_b2"], dtype=np.float32)[core],
             np.asarray(inputs["L0_ffn_sb2"], dtype=np.float32)]).reshape(1, 2 * E))
        in_maps.append({
            "x": np.ascontiguousarray(x[t0:t0 + T]),
            "wq": Wq, "wk": Wk, "wv": Wv, "wqp": Wqp, "wkp": Wkp, "wo": Wo,
            "wg1": np.ascontiguousarray(
                np.asarray(inputs["L0_attn_Wg1"], dtype=np.float32)),
            "wg2": np.ascontiguousarray(
                np.asarray(inputs["L0_attn_Wg2"], dtype=np.float32)),
            "gkc": gkc, "bkc": bkc,
            "rwt": np.ascontiguousarray(
                np.asarray(inputs["L0_ffn_rw"], dtype=np.float32).T),
            "rb": np.asarray(inputs["L0_ffn_rb"], dtype=np.float32).reshape(1, 8),
            "w1m": np.ascontiguousarray(
                np.asarray(inputs["L0_ffn_w1"], dtype=np.float32)[core]),
            "w2m": np.ascontiguousarray(
                np.asarray(inputs["L0_ffn_w2"], dtype=np.float32)[core]),
            "sw1": np.ascontiguousarray(
                np.asarray(inputs["L0_ffn_sw1"], dtype=np.float32)),
            "sw2": np.ascontiguousarray(
                np.asarray(inputs["L0_ffn_sw2"], dtype=np.float32)),
            "b1s": b1mat, "b2x": b2row, "csin": csin, "pmask": pm,
            "intidx": iix, "intidx2": iix2, "tids": tid,
        })
    return in_maps


def kernel(**inputs):
    in_maps = _host_inputs(inputs)
    nc = _build()
    res = run_bass_kernel_spmd(nc, in_maps, core_ids=list(range(NCORE)))
    out = np.concatenate([res.results[c]["out"] for c in range(NCORE)], axis=0)
    return np.ascontiguousarray(out.reshape(2, 8192, E))


# version 14 (final)
# speedup vs baseline: 1.0980x; 1.0980x over previous
import numpy as np
import ml_dtypes
import concourse.bass as bass
import concourse.mybir as mybir
from concourse.bass import IndirectOffsetOnAxis
from concourse.bass_utils import run_bass_kernel_spmd
from contextlib import ExitStack

F32 = mybir.dt.float32
BF16 = mybir.dt.bfloat16
I32 = mybir.dt.int32
AF = mybir.ActivationFunctionType
ALU = mybir.AluOpType
AX = mybir.AxisListType.X
BF = ml_dtypes.bfloat16

NCORE = 8
T = 2048
NT = 16384
E = 1024
H = 16
D = 64
FF = 4096
CAP = 2048
FBPC = 1024
FBG = NCORE * FBPC
ROWS = CAP + FBPC
NCH = 16
TBLN = NT + FBG
LGW = 1e30

ENGS = ("sync", "scalar", "vector", "tensor", "gpsimd")


class Bld:
    def __init__(self):
        self.q = {e: [] for e in ENGS}
        self.cnt = {}

    def sem(self, *names):
        for n in names:
            self.cnt[n] = 0

    def op(self, eng, fn, waits=(), inc=None, inc_by=1):
        waits = list(waits)
        self_sem = {"vector": "cV", "scalar": "cA"}.get(eng)
        if self_sem is not None:
            waits.append((self_sem, self.cnt[self_sem]))
        merged = {}
        for s, v in waits:
            if s.startswith("q"):
                v = self.cnt[s]
            merged[s] = max(merged.get(s, 0), v)
        self.q[eng].append((tuple(merged.items()), fn, inc, inc_by))
        if inc is not None:
            self.cnt[inc] += inc_by

    def n(self, sem):
        return self.cnt[sem]


def _build(dump=None):
    dump = dump or []
    nc = bass.Bass("TRN2", target_bir_lowering=False, debug=False)

    def din(name, shape, dt=F32):
        return nc.dram_tensor(name, shape, dt, kind="ExternalInput").ap()

    x_in = din("x", [T, E])
    wq = din("wq", [E, E]); wk = din("wk", [E, E]); wv = din("wv", [E, E])
    wqp = din("wqp", [E, E]); wkp = din("wkp", [E, E]); wo = din("wo", [E, E])
    wg1 = din("wg1", [E, 64]); wg2 = din("wg2", [64, E])
    gkc = din("gkc", [128, 8]); bkc = din("bkc", [128, 8])
    rwt = din("rwt", [E, 8]); rb = din("rb", [1, 8])
    w1m = din("w1m", [E, FF]); w2m = din("w2m", [FF, E])
    sw1 = din("sw1", [E, FF]); sw2 = din("sw2", [FF, E])
    b1s = din("b1s", [128, 64])
    b2x = din("b2x", [1, 2 * E])
    csin = din("csin", [96, T])
    pmask = din("pmask", [1, 8])
    intidx = din("intidx", [128, 8], I32)
    intidx2 = din("intidx2", [128, 24], I32)
    tids = din("tids", [128, 16], I32)

    out_t = nc.dram_tensor("out", [T, E], F32, kind="ExternalOutput").ap()

    def dint(name, shape, dt=F32):
        kind = "ExternalOutput" if name in dump else "Internal"
        return nc.dram_tensor(name, shape, dt, kind=kind).ap()

    xnag_in = dint("xnag_in", [T, E], BF16)
    xnall = nc.dram_tensor("xnall", [NT, E], BF16, kind="Internal").ap()
    lg_in = dint("lg_in", [T, 8])
    lg_all = nc.dram_tensor("lg_all", [NT, 8], F32, kind="Internal").ap()
    sl_in = dint("sl_in", [H * D, D])
    sl_all = nc.dram_tensor("sl_all", [NCORE * H * D, D], F32, kind="Internal").ap()
    tbl_in = nc.dram_tensor("tbl_in", [TBLN, 1], I32, kind="Internal").ap()
    tbl_all = nc.dram_tensor("tbl_all", [TBLN, 1], I32, kind="Internal").ap()
    ymine = dint("ymine", [ROWS, E], BF16)
    yall = nc.dram_tensor("yall", [NCORE * ROWS, E], BF16,
                          kind="Internal").ap()
    sw1d = nc.dram_tensor("sw1d", [E, FF], BF16, kind="Internal").ap()
    sw2d = nc.dram_tensor("sw2d", [FF, E], BF16, kind="Internal").ap()
    spark = dint("spark", [T, E])
    idxall = dint("idxall", [NT, 1], I32)
    idxy = dint("idxy", [NT, 1], I32)
    bnc = nc.dram_tensor("bnc", [8, E], F32, kind="Internal").ap()
    ogd = nc.dram_tensor("ogd", [T, E], F32, kind="Internal").ap()
    attd = nc.dram_tensor("attd", [T, E], BF16, kind="Internal").ap()
    dbgd = {}
    for nm, shp, dt in (("d_xnall", [NT, E], BF16), ("d_lgall", [NT, 8], F32),
                        ("d_slall", [NCORE * H * D, D], F32),
                        ("d_tbl", [TBLN, 1], I32),
                        ("d_yall", [NCORE * ROWS, E], BF16)):
        if nm in dump:
            dbgd[nm] = nc.dram_tensor(nm, shp, dt, kind="ExternalOutput").ap()

    c_idbf = nc.inline_tensor(np.eye(128, dtype=BF), "c_idbf").ap()
    c_idf = nc.inline_tensor(np.eye(128, dtype=np.float32), "c_idf").ap()
    c_mask = nc.inline_tensor(np.triu(np.ones((128, 128), np.float32)), "c_mask").ap()
    c_trub = nc.inline_tensor(np.triu(np.ones((128, 128), BF), 1), "c_trub").ap()
    c_truf = nc.inline_tensor(np.triu(np.ones((128, 128), np.float32), 1),
                              "c_truf").ap()
    c_onecb = nc.inline_tensor(np.ones((128, 1), BF), "c_onecb").ap()
    c_onecf = nc.inline_tensor(np.ones((128, 1), np.float32), "c_onecf").ap()
    c_onerf = nc.inline_tensor(np.ones((1, 128), np.float32), "c_onerf").ap()
    c_zcb = nc.inline_tensor(np.zeros((1, 128), BF), "c_zcb").ap()
    c_onerb = nc.inline_tensor(np.ones((1, 512), BF), "c_onerb").ap()
    c_zroi = nc.inline_tensor(np.zeros((128, TBLN // 128), np.int32), "c_zroi").ap()
    c_vals = nc.inline_tensor(
        np.tile(np.array([[1e-5, -0.5, 0.0, 1.0, 2.0, 0.5, -1.0, 9.9e5]],
                         np.float32), (128, 1)), "c_vals").ap()

    B = Bld()
    B.sem("qS", "qA", "qG", "qI", "qO", "cP", "cV", "cA", "cG", "cc",
          "qL", "qXN", "qSL", "qY", "qJ", "qX0", "qX1", "qD0", "qD1",
          "qW0", "qW1", "qYA", "qYB", "qYC", "qZ0", "qZ1")
    bc_regs = {}

    def dma(eng, out, in_, sem, waits=(), transpose=False):
        if transpose:
            B.op(eng, (lambda o=out, i=in_: lambda e: e.dma_start(
                out=o, in_=i, transpose=True))(), waits=waits, inc=sem, inc_by=16)
        else:
            B.op(eng, (lambda o=out, i=in_: lambda e: e.dma_start(
                out=o, in_=i))(), waits=waits, inc=sem, inc_by=16)
        return B.n(sem)

    with ExitStack() as ctx:
        en = ctx.enter_context

        def sb(name, shape, dt):
            return en(nc.sbuf_tensor(name, shape, dt))

        PS_A = en(nc.psum_tensor("PS_A", [128, 512], F32))
        PS_B = en(nc.psum_tensor("PS_B", [128, 512], F32))
        PS_C = en(nc.psum_tensor("PS_C", [128, 512], F32))
        PS_E = en(nc.psum_tensor("PS_E", [128, 1024], F32))
        PS_F = en(nc.psum_tensor("PS_F", [128, 1024], F32))
        PS_TB = en(nc.psum_tensor("PS_TB", [128, 1024], BF16))

        ABT = sb("ABT", [128, 86016], BF16)
        AFT = sb("AFT", [128, 8192], F32)
        QFM = ABT[:, 0:16384]
        KFM = ABT[:, 16384:32768]
        KTM = ABT[:, 32768:49152]
        VTM = ABT[:, 49152:65536]
        WBUF = ABT[:, 65536:81920]
        XNTF = ABT[:, 32768:49152]
        WG1S = ABT[:, 81920:82432]
        WOB = ABT[:, 65536:73728]
        XNB = ABT[:, 0:2048]
        RB16 = ABT[:, 84480:85504]
        SPB0 = ABT[:, 73728:74752]
        SPB1 = ABT[:, 74752:75776]
        STS = ABT[:, 75776:76800]
        SOP = ABT[:, 76800:77824]
        G1FM = ABT[:, 82432:84480]
        ETM = ABT[:, 79872:80384]
        WG2S = ABT[:, 80384:81408]
        W1B = ABT[:, 32768:65536]
        W2B = ABT[:, 0:32768]
        HSB = ABT[:, 65536:81920]
        XG = ABT[:, 81920:86016]
        XGR = ABT[:, 65536:69632]
        XT = AFT[:, 0:2048]
        XNT = AFT[:, 2048:4096]
        TABS = AFT[:, 0:1024]
        TABSB = AFT[:, 3072:4096]
        TQ = AFT[:, 1024:1536]
        TQP = AFT[:, 1536:2048]
        LG = AFT[:, 0:1024]
        R1 = AFT[:, 1024:2048]
        R2 = AFT[:, 2048:3072]
        R3 = AFT[:, 3072:4096]
        R4 = AFT[:, 4096:5120]
        R5 = AFT[:, 5120:6144]
        RS = AFT[:, 6144:7680]
        GATE = AFT[:, 0:1024]
        OG = AFT[:, 1024:2048]
        SPRE = AFT[:, 2048:3072]
        SOUT = AFT[:, 3072:5120]
        SPREF = AFT[:, 5120:6144]
        XLP = AFT[:, 4096:5120]
        YEV = AFT[:, 0:2048]
        B2BC = AFT[:, 2048:4096]
        B2ROW = AFT[0:1, 4096:6144]
        BWOB0 = ABT[0:1, 81920:82432]
        BWOB1 = ABT[0:1, 85504:86016]
        SST = AFT[:, 7168:8192]
        YGB = ABT[:, 0:2048]
        SLD = AFT[:, 2048:4096]
        XLD = AFT[:, 4096:6144]
        OUTT = AFT[:, 6144:8192]

        RWS = sb("RWS", [128, 64], F32)
        RBBC = sb("RBBC", [128, 8], F32)
        PMB = sb("PMB", [128, 8], F32)
        GKC = sb("GKC", [128, 8], F32)
        BKC = sb("BKC", [128, 8], F32)
        B1SS = sb("B1SS", [128, 64], F32)
        IIX = sb("IIX", [128, 8], I32)
        IIX2 = sb("IIX2", [128, 24], I32)
        TIDS = sb("TIDS", [128, 16], I32)
        SMALL = sb("SMALL", [128, 48], F32)
        RCE = sb("RCE", [128, 64], F32)
        IDXI = sb("IDXI", [128, 128], I32)
        OWNW = sb("OWNW", [128, 16], I32)
        OWNY = sb("OWNY", [128, 16], I32)
        TBLW = sb("TBLW", [128, 24], I32)
        TBLF = sb("TBLF", [128, 24], F32)
        IDBF = sb("IDBF", [128, 128], BF16)
        IDF = sb("IDF", [128, 128], F32)
        MSK = sb("MSK", [128, 128], F32)
        TRB = sb("TRB", [128, 128], BF16)
        TRF = sb("TRF", [128, 128], F32)
        OCB = sb("OCB", [128, 1], BF16)
        OCF = sb("OCF", [128, 1], F32)
        ORF = sb("ORF", [1, 128], F32)
        ZCB = sb("ZCB", [1, 128], BF16)
        ORB = sb("ORB", [1, 512], BF16)
        RBROW = sb("RBROW", [1, 8], F32)
        CASB = sb("CASB", [128, 8], F32)
        BKCB = sb("BKCB", [128, 8], BF16)
        PMROW = sb("PMROW", [1, 8], F32)

        for i_, v_ in enumerate([1e-5, -0.5, 0.0, 1.0, 2.0, 0.5, -1.0, 9.9e5]):
            nc.const_aps.register(F32, v_, CASB[:, i_:i_ + 1])

        semh = {}
        for s_ in ("qS", "qA", "qG", "qI", "qO", "cP", "cV", "cA", "cG", "cc",
                   "qL", "qXN", "qSL", "qY", "qJ", "qX0", "qX1", "qD0", "qD1",
                   "qW0", "qW1", "qYA", "qYB", "qYC", "qZ0", "qZ1"):
            semh[s_] = en(nc.semaphore(s_))

        for dst, src in ((IIX[:, :], intidx), (IIX2[:, :], intidx2),
                         (TIDS[:, :], tids),
                         (B1SS[:, :], b1s), (IDBF[:, :], c_idbf),
                         (IDF[:, :], c_idf), (MSK[:, :], c_mask),
                         (TRB[:, :], c_trub), (TRF[:, :], c_truf),
                         (OCB[:, :], c_onecb), (OCF[:, :], c_onecf),
                         (ORF[:, :], c_onerf), (ZCB[:, :], c_zcb),
                         (ORB[:, :], c_onerb), (RBROW[:, :], rb),
                         (PMROW[:, :], pmask), (GKC[:, :], gkc),
                         (BKC[:, :], bkc), (CASB[:, :], c_vals)):
            dma("sync", dst, src, "qS")
        dma("sync", RWS.rearrange("p (kc e) -> p kc e", e=8),
            rwt.rearrange("(kc p) e -> p kc e", p=128), "qS")
        dma("sync", tbl_in.rearrange("(p f) one -> p (f one)", p=128), c_zroi, "qS")
        _pre_qS = B.n("qS")
        _pre_qG = B.n("qG")
        dma("gpsimd", sw1d, sw1, "qG")
        dma("gpsimd", sw2d, sw2, "qG")
        _swd_qG = B.n("qG")
        for dst, src in ((RBBC, RBROW[:, 0:8]), (PMB, PMROW[:, 0:8])):
            B.op("tensor", (lambda s=src: lambda t: t.matmul(
                PS_A[:, 0:8], ORF[:, :], s, start=True, stop=True))(),
                waits=[("qS", _pre_qS), ("cV", B.n("cV"))], inc="cP")
            B.op("vector", (lambda d=dst: lambda v: v.tensor_copy(
                d[:, 0:8], PS_A[:, 0:8]))(),
                waits=[("cP", B.n("cP"))], inc="cV")
        _bc_cV = B.n("cV")
        _bc_cP = B.n("cP")

        XNT3 = XNTF.rearrange("p (kc t) -> p kc t", t=2048)
        PT3B = PS_TB.rearrange("p (kc n) -> p kc n", n=128)
        x_tiles = x_in.rearrange("(c p) e -> c p e", p=128)
        xnag_tiles = xnag_in.rearrange("(c p) e -> c p e", p=128)
        lg_tiles = lg_in.rearrange("(c p) e -> c p e", p=128)
        ph1_v = {}
        ph1_p = {}
        ph1_a = {}
        for c in range(NCH):
            s0 = (c % 2) * 1024
            so = (c % 2) * 16
            sl = slice(s0, s0 + 1024)
            if c >= 2:
                wg_ = [("cV", ph1_v[c - 2]), ("cP", ph1_p[c - 2]),
                       ("qXN", ph1_a[c - 2])]
            else:
                wg_ = [("cV", _bc_cV), ("cP", _bc_cP)]
            qx = "qX0" if c % 2 == 0 else "qX1"
            _xq = dma("sync", XT[:, sl], x_tiles[c, :, :], qx, waits=wg_)
            B.op("vector", (lambda s_=sl, o_=so: lambda v: v.reduce_sum(
                SMALL[:, o_:o_ + 1], XT[:, s_], axis=AX))(),
                waits=[(qx, _xq)], inc="cV")
            B.op("scalar", (lambda s_=sl, o_=so: lambda s: s.activation(
                XNT[:, s_], XT[:, s_], AF.Square,
                accum_out=SMALL[:, o_ + 1:o_ + 2]))(),
                waits=[(qx, _xq), ("cV", B.n("cV"))], inc="cA")
            B.op("vector", (lambda o_=so: lambda v: v.tensor_scalar(
                SMALL[:, o_ + 2:o_ + 3], SMALL[:, o_:o_ + 1],
                1.0 / E, None, ALU.mult))(),
                waits=[("cA", B.n("cA"))], inc="cV")
            B.op("vector", (lambda o_=so: lambda v: v.tensor_mul(
                SMALL[:, o_ + 3:o_ + 4], SMALL[:, o_ + 2:o_ + 3],
                SMALL[:, o_ + 2:o_ + 3]))(), inc="cV")
            B.op("vector", (lambda o_=so: lambda v: v.tensor_scalar(
                SMALL[:, o_ + 4:o_ + 5], SMALL[:, o_ + 1:o_ + 2],
                1.0 / E, None, ALU.mult))(), inc="cV")
            B.op("vector", (lambda o_=so: lambda v: v.tensor_sub(
                SMALL[:, o_ + 4:o_ + 5], SMALL[:, o_ + 4:o_ + 5],
                SMALL[:, o_ + 3:o_ + 4]))(), inc="cV")
            B.op("scalar", (lambda o_=so: lambda s: s.activation(
                SMALL[:, o_ + 5:o_ + 6], SMALL[:, o_ + 4:o_ + 5],
                AF.Ln, bias=1e-5, scale=1.0))(),
                waits=[("cV", B.n("cV"))], inc="cA")
            B.op("scalar", (lambda o_=so: lambda s: s.activation(
                SMALL[:, o_ + 6:o_ + 7], SMALL[:, o_ + 5:o_ + 6],
                AF.Exp, bias=0.0, scale=-0.5))(), inc="cA")
            B.op("vector", (lambda s_=sl, o_=so: lambda v: v.tensor_scalar(
                XNT[:, s_], XT[:, s_], SMALL[:, o_ + 2:o_ + 3],
                SMALL[:, o_ + 6:o_ + 7], ALU.subtract, ALU.mult))(),
                waits=[("cA", B.n("cA"))], inc="cV")
            _xn_cV = B.n("cV")
            B.op("vector", (lambda s_=sl: lambda v: v.tensor_copy(
                XNB[:, s_], XNT[:, s_]))(), inc="cV")
            dma("scalar", xnag_tiles[c, :, :], XNB[:, sl], "qXN",
                waits=[("cV", B.n("cV"))])
            for j in range(8):
                B.op("tensor", (lambda s_=s0, j_=j: lambda t: t.transpose(
                    PS_TB[:, j_ * 128:j_ * 128 + 128],
                    XNB[:, s_ + j_ * 128:s_ + j_ * 128 + 128], IDBF[:, :]))(),
                    waits=([("cV", B.n("cV"))] if j == 0 else []),
                    inc=("cP" if j == 7 else None))
            B.op("vector", (lambda c_=c: lambda v: v.tensor_copy(
                XNT3[:, :, c_ * 128:c_ * 128 + 128], PT3B[:, :, :]))(),
                waits=[("cP", B.n("cP"))], inc="cV")
            rs_cv = {}
            for g in range(2):
                for j4 in range(4):
                    j = g * 4 + j4
                    B.op("tensor", (lambda s_=sl, j_=j: lambda t: t.transpose(
                        PS_B[:, (j_ % 4) * 128:(j_ % 4) * 128 + 128],
                        XNT[:, s_.start + j_ * 128:s_.start + j_ * 128 + 128],
                        IDF[:, :]))(),
                        waits=([("cV", max(_xn_cV, B.n("cV")))]
                               if j4 == 0 else []),
                        inc=("cP" if j4 == 3 else None))
                B.op("vector", (lambda g_=g: lambda v: v.tensor_copy(
                    RS[:, g_ * 512:g_ * 512 + 512], PS_B[:, 0:512]))(),
                    waits=[("cP", B.n("cP"))], inc="cV")
                rs_cv[g] = B.n("cV")
                for j4 in range(4):
                    j = g * 4 + j4
                    B.op("tensor", (lambda j_=j: lambda t: t.matmul(
                        PS_C[:, 0:8], RS[:, j_ * 128:j_ * 128 + 128],
                        RWS[:, j_ * 8:j_ * 8 + 8],
                        start=(j_ == 0), stop=(j_ == 7)))(),
                        waits=([("cV", rs_cv[g])] if j4 == 0 else []),
                        inc=("cP" if j == 7 else None))
            B.op("vector", lambda v: v.scalar_tensor_tensor(
                RS[:, 1024:1032], PS_C[:, 0:8], 1.0, RBBC[:, 0:8],
                ALU.mult, ALU.add),
                waits=[("cP", B.n("cP")), ("qL", B.n("qL"))], inc="cV")
            dma("scalar", lg_tiles[c, :, :], RS[:, 1024:1032], "qL",
                waits=[("cV", B.n("cV"))])
            ph1_v[c] = B.n("cV")
            ph1_p[c] = B.n("cP")
            ph1_a[c] = B.n("qXN")

        wk_ok = ("qW0", dma("gpsimd",
                 WBUF[:, 0:8192].rearrange("p (kc f) -> p kc f", f=1024),
                 wk.rearrange("(kc p) f -> p kc f", p=128), "qW0",
                 waits=[("qG", _pre_qG)]))
        wkp_ok = ("qW1", dma("gpsimd",
                  WBUF[:, 8192:16384].rearrange("p (kc f) -> p kc f", f=1024),
                  wkp.rearrange("(kc p) f -> p kc f", p=128), "qW1",
                  waits=[]))
        B.op("gpsimd", lambda g: g.collective_compute(
            "AllGather", ALU.bypass, replica_groups=[list(range(NCORE))],
            ins=[lg_in], outs=[lg_all]),
            waits=[("qL", B.n("qL"))], inc="cc")
        _cc_lg = B.n("cc")
        B.op("gpsimd", lambda g: g.collective_compute(
            "AllGather", ALU.bypass, replica_groups=[list(range(NCORE))],
            ins=[xnag_in], outs=[xnall]),
            waits=[("qXN", B.n("qXN"))], inc="cc")
        _cc_xn = B.n("cc")

        def w_into(slot, src, waits):
            return (("qW0", dma("gpsimd",
                    WBUF[:, slot * 8192:(slot + 1) * 8192].rearrange(
                        "p (kc f) -> p kc f", f=1024),
                    src.rearrange("(kc p) f -> p kc f", p=128), "qW0",
                    waits=waits)) if slot == 0 else
                    ("qW1", dma("gpsimd",
                     WBUF[:, slot * 8192:(slot + 1) * 8192].rearrange(
                         "p (kc f) -> p kc f", f=1024),
                     src.rearrange("(kc p) f -> p kc f", p=128), "qW1",
                     waits=waits)))

        TABSD = (TABS, TABSB)

        def load_tabs(nc_i, waits, buf=0):
            last = 0
            s_ = slice(nc_i * 512, (nc_i + 1) * 512)
            for i, (qtr, r0, col) in enumerate(
                    [(0, 0, 0), (1, 0, 0), (2, 0, 0), (3, 0, 0),
                     (0, 32, 512), (1, 64, 512), (2, 32, 512), (3, 64, 512)]):
                last = dma("sync",
                           TABSD[buf][qtr * 32:qtr * 32 + 32, col:col + 512],
                           csin[r0:r0 + 32, s_], "qS",
                           waits=(waits if i == 0 else ()))
            return last

        def fm_pass(dst, wqg_pair, slots=(0, 1)):
            tok = {0: load_tabs(0, [("cV", B.n("cV"))], buf=0)}
            for nc_i in range(4):
                if nc_i + 1 < 4:
                    tok[nc_i + 1] = load_tabs(nc_i + 1, [("cV", B.n("cV"))],
                                              buf=(nc_i + 1) % 2)
                tb_b = TABSD[nc_i % 2]
                for ft in range(8):
                    for ps, wsl, tmp in ((PS_A, slots[0], TQ),
                                         (PS_B, slots[1], TQP)):
                        for kc in range(8):
                            B.op("tensor",
                                 (lambda p_=ps, w_=wsl, k_=kc, f_=ft,
                                  n_=nc_i: lambda t:
                                  t.matmul(p_[:, 0:512],
                                           WBUF[:, w_ * 8192 + k_ * 1024 + f_ * 128:
                                                w_ * 8192 + k_ * 1024 + f_ * 128 + 128],
                                           XNTF[:, k_ * 2048 + n_ * 512:
                                                k_ * 2048 + n_ * 512 + 512],
                                           start=(k_ == 0), stop=(k_ == 7)))(),
                                 waits=([wqg_pair[0], wqg_pair[1],
                                         ("cA", B.n("cA")),
                                         ("cV", B.n("cV"))] if kc == 0 else []),
                                 inc=("cP" if kc == 7 else None))
                        B.op("scalar",
                             (lambda p_=ps, t_=tmp: lambda s: s.activation(
                                 t_[:, :], p_[:, 0:512], AF.Silu))(),
                             waits=[("cP", B.n("cP")), ("cV", B.n("cV"))],
                             inc="cA")
                    dsl = dst[:, ft * 2048 + nc_i * 512:ft * 2048 + nc_i * 512 + 512]
                    B.op("vector", (lambda t_=tb_b: lambda v: v.tensor_mul(
                        TQ[:, :], TQ[:, :], t_[:, 0:512]))(),
                        waits=[("cA", B.n("cA")), ("qS", tok[nc_i])], inc="cV")
                    B.op("vector", (lambda t_=tb_b: lambda v: v.tensor_mul(
                        TQP[:, :], TQP[:, :], t_[:, 512:1024]))(), inc="cV")
                    B.op("vector", (lambda d=dsl: lambda v: v.tensor_add(
                        d, TQ[:, :], TQP[:, :]))(), inc="cV")


        fm_pass(KFM, (wk_ok, wkp_ok))
        _k_cV = B.n("cV")

        wv_ok = w_into(0, wv, [("cP", B.n("cP"))])
        wg1f_ok = ("qW0", dma("gpsimd",
                   WG1S.rearrange("p (kc f) -> p kc f", f=64),
                   wg1.rearrange("(kc p) f -> p kc f", p=128), "qW0",
                   waits=[]))
        wq_ok = None
        for nc_i in range(4):
            if nc_i == 1:
                wq_ok = w_into(1, wq, [("cP", B.n("cP"))])
            for mt in range(4):
                ch = nc_i * 4 + mt
                for half in range(2):
                    ps = PS_A if half == 0 else PS_B
                    for kc in range(8):
                        B.op("tensor",
                             (lambda p_=ps, k_=kc, m_=mt, h_=half,
                              n_=nc_i: lambda t:
                              t.matmul(p_[:, 0:512],
                                       XNTF[:, k_ * 2048 + n_ * 512 + m_ * 128:
                                            k_ * 2048 + n_ * 512 + m_ * 128 + 128],
                                       WBUF[:, k_ * 1024 + h_ * 512:
                                            k_ * 1024 + h_ * 512 + 512],
                                       start=(k_ == 0), stop=(k_ == 7)))(),
                             waits=([wv_ok,
                                     ("cV", B.n("cV"))] if kc == 0 else []),
                             inc=("cP" if kc == 7 else None))
                    B.op("vector",
                         (lambda p_=ps, c_=ch, h_=half: lambda v: v.tensor_copy(
                             VTM[:, c_ * 1024 + h_ * 512:c_ * 1024 + h_ * 512 + 512],
                             p_[:, 0:512]))(),
                         waits=[("cP", B.n("cP"))], inc="cV")
            for kc in range(8):
                B.op("tensor", (lambda k_=kc, n_=nc_i: lambda t:
                     t.matmul(PS_C[0:64, 0:512],
                              WG1S[:, k_ * 64:k_ * 64 + 64],
                              XNTF[:, k_ * 2048 + n_ * 512:
                                   k_ * 2048 + n_ * 512 + 512],
                              start=(k_ == 0), stop=(k_ == 7)))(),
                     waits=([wg1f_ok,
                             ("cV", B.n("cV"))] if kc == 0 else []),
                     inc=("cP" if kc == 7 else None))
            B.op("vector", (lambda n_=nc_i: lambda v: v.tensor_copy(
                G1FM[0:64, n_ * 512:(n_ + 1) * 512],
                PS_C[0:64, 0:512]))(),
                waits=[("cP", B.n("cP"))], inc="cV")
        _v_cV = B.n("cV")
        if dump:
            dma("scalar", attd[0:128, :].rearrange("(a p) e -> p a e", p=64),
                G1FM[0:64, :].rearrange("p (a e) -> p a e", a=2), "qA",
                waits=[("cV", B.n("cV"))])

        wqp_ok = w_into(0, wqp, [("cP", B.n("cP"))])
        fm_pass(QFM, (wq_ok, wqp_ok), slots=(1, 0))
        _q_cV = B.n("cV")
        wo_ok = dma("gpsimd",
                    WOB.rearrange("p (kc f) -> p kc f", f=1024),
                    wo.rearrange("(kc p) f -> p kc f", p=128), "qW0",
                    waits=[("cP", B.n("cP")), ("cV", B.n("cV"))])
        wg2_ok = dma("gpsimd", WG2S[0:64, :], wg2, "qW1")

        for c in range(NCH):
            for ft in range(8):
                B.op("tensor", (lambda c_=c, f_=ft: lambda t: t.transpose(
                    PS_TB[:, f_ * 128:f_ * 128 + 128],
                    KFM[:, f_ * 2048 + c_ * 128:f_ * 2048 + c_ * 128 + 128],
                    IDBF[:, :]))(),
                    waits=([("cV", max(_k_cV, B.n("cV")))] if ft == 0 else []),
                    inc=("cP" if ft == 7 else None))
            B.op("vector", (lambda c_=c: lambda v: v.tensor_copy(
                KTM[:, c_ * 1024:(c_ + 1) * 1024], PS_TB[:, :]))(),
                waits=[("cP", B.n("cP"))], inc="cV")
        _ktm_cV = B.n("cV")

        B.op("vector", lambda v: v.tensor_copy(BKCB[:, :], BKC[:, :]),
             waits=[("qS", _pre_qS)], inc="cV")
        _bkcb_cV = B.n("cV")
        for kc in range(8):
            for hf in range(2):
                B.op("tensor", (lambda k_=kc, h_=hf: lambda t: t.matmul(
                    PS_F[0:1, h_ * 512:(h_ + 1) * 512],
                    BKCB[:, k_:k_ + 1],
                    WOB[:, k_ * 1024 + h_ * 512:k_ * 1024 + h_ * 512 + 512],
                    start=(k_ == 0), stop=(k_ == 7)))(),
                    waits=([("qW0", wo_ok), ("cV", _bkcb_cV)]
                           if (kc == 0 and hf == 0) else []),
                    inc=("cP" if (kc == 7 and hf == 1) else None))
        B.op("vector", lambda v: v.tensor_copy(BWOB0[0:1, :],
                                               PS_F[0:1, 0:512]),
             waits=[("cP", B.n("cP"))], inc="cV")
        B.op("vector", lambda v: v.tensor_copy(BWOB1[0:1, :],
                                               PS_F[0:1, 512:1024]),
             inc="cV")
        _bwo_cV = B.n("cV")
        for kc in range(8):
            B.op("vector", (lambda k_=kc: lambda v: v.tensor_scalar(
                WOB[:, k_ * 1024:(k_ + 1) * 1024],
                WOB[:, k_ * 1024:(k_ + 1) * 1024],
                GKC[:, k_:k_ + 1], None, ALU.mult))(),
                waits=([("cP", B.n("cP"))] if kc == 0 else []), inc="cV")
        _wof_cV = B.n("cV")

        for h in range(H):
            for c in range(NCH):
                B.op("tensor", (lambda h_=h, c_=c: lambda t: t.matmul(
                    PS_F[0:64, h_ * 64:(h_ + 1) * 64],
                    KTM[:, c_ * 1024 + h_ * 64:c_ * 1024 + h_ * 64 + 64],
                    VTM[:, c_ * 1024 + h_ * 64:c_ * 1024 + h_ * 64 + 64],
                    start=(c_ == 0), stop=(c_ == NCH - 1)))(),
                    waits=([("cV", max(_ktm_cV, _v_cV, _bwo_cV))]
                           if (h == 0 and c == 0) else []),
                    inc=("cP" if c == NCH - 1 else None))
        B.op("vector", lambda v: v.tensor_copy(SPRE[0:64, :], PS_F[0:64, :]),
             waits=[("cP", B.n("cP"))], inc="cV")
        dma("sync", sl_in.rearrange("(h dk) dv -> dk h dv", dk=64),
            SPRE[0:64, :].rearrange("p (h dv) -> p h dv", dv=64), "qSL",
            waits=[("cV", B.n("cV"))])
        B.op("gpsimd", lambda g: g.collective_compute(
            "AllGather", ALU.bypass, replica_groups=[list(range(NCORE))],
            ins=[sl_in], outs=[sl_all]),
            waits=[("qSL", B.n("qSL"))], inc="cc")
        _cc_sl = B.n("cc")


        _lg_ok = dma("sync", LG.rearrange("p (c e) -> p c e", e=8),
                     lg_all.rearrange("(c p) e -> p c e", p=128), "qS",
                     waits=[("cc", _cc_lg), ("cV", B.n("cV")),
                            ("cA", B.n("cA"))])

        def _r(fn, waits=None):
            B.op("vector", fn, waits=(waits or []), inc="cV")

        def rs(i):
            return RS[:, i * 128:(i + 1) * 128]

        def rc(i):
            return RCE[:, i * 8:(i + 1) * 8]

        def d3(buf):
            return buf.rearrange("p (c e) -> p c e", e=8)

        def bcl(col2d):
            return col2d.rearrange("p (c one) -> p c one", one=1) \
                        .to_broadcast([128, 128, 8])

        def first_max_onehot(buf, maxsl):
            b3 = d3(buf)
            R43_ = d3(R4)
            R53_ = d3(R5)
            _r(lambda v: v.reduce_max(maxsl, b3, axis=AX))
            _r(lambda v: v.tensor_tensor(b3, b3, bcl(maxsl), ALU.is_ge))
            _r(lambda v: v.memset(R4[:, :], 0.0))
            _r(lambda v: v.tensor_copy(R43_[:, :, 1:8], b3[:, :, 0:7]))
            _r(lambda v: v.memset(R5[:, :], 0.0))
            _r(lambda v: v.tensor_copy(R53_[:, :, 2:8], R43_[:, :, 0:6]))
            _r(lambda v: v.tensor_add(R4[:, :], R4[:, :], R5[:, :]))
            _r(lambda v: v.memset(R5[:, :], 0.0))
            _r(lambda v: v.tensor_copy(R53_[:, :, 4:8], R43_[:, :, 0:4]))
            _r(lambda v: v.tensor_add(R4[:, :], R4[:, :], R5[:, :]))
            _r(lambda v: v.tensor_scalar(R4[:, :], R4[:, :], 0.0, None,
                                         ALU.is_equal))
            _r(lambda v: v.tensor_mul(buf[:, :], buf[:, :], R4[:, :]))

        def cumsum_block(out_sb, tot_slot, car_slot, bnc_row):
            for hf in range(2):
                B.op("tensor", (lambda h_=hf: lambda t: t.matmul(
                    PS_E[:, h_ * 512:(h_ + 1) * 512], TRB[:, :],
                    RB16[:, h_ * 512:(h_ + 1) * 512], start=True, stop=True))(),
                    waits=([("cV", B.n("cV"))] if hf == 0 else []), inc=None)
            for hf in range(2):
                B.op("tensor", (lambda h_=hf: lambda t: t.matmul(
                    PS_F[0:1, h_ * 512:(h_ + 1) * 512], OCB[:, :],
                    RB16[:, h_ * 512:(h_ + 1) * 512], start=True, stop=True))(),
                    inc=("cP" if hf == 1 else None))
            B.op("vector", lambda v: v.tensor_copy(R4[0:1, :], PS_F[0:1, :]),
                 waits=[("cP", B.n("cP"))], inc="cV")
            dma("sync", bnc[bnc_row:bnc_row + 1, :], R4[0:1, :], "qS",
                waits=[("cV", B.n("cV"))])
            _t_ok = dma("sync", rc(tot_slot),
                        bnc[bnc_row:bnc_row + 1, :].rearrange(
                            "one (c e) -> (one c) e", e=8), "qS",
                        waits=[("qS", 0)])
            B.op("tensor", (lambda ts=tot_slot: lambda t: t.matmul(
                PS_C[:, 0:8], TRF[:, :], rc(ts), start=True, stop=True))(),
                waits=[("qS", _t_ok), ("cV", B.n("cV"))], inc="cP")
            B.op("vector", (lambda cs=car_slot: lambda v: v.tensor_copy(
                rc(cs), PS_C[:, 0:8]))(),
                waits=[("cP", B.n("cP"))], inc="cV")
            dma("sync", bnc[bnc_row + 1:bnc_row + 2, :].rearrange(
                "one (c e) -> (one c) e", e=8), rc(car_slot), "qS",
                waits=[("cV", B.n("cV"))])
            _c_ok = dma("sync", R4[0:1, :], bnc[bnc_row + 1:bnc_row + 2, :],
                        "qS", waits=[("qS", 0)])
            for hf in range(2):
                B.op("tensor", (lambda h_=hf: lambda t: t.matmul(
                    PS_F[:, h_ * 512:(h_ + 1) * 512], ORF[:, :],
                    R4[0:1, h_ * 512:(h_ + 1) * 512], start=True, stop=True))(),
                    waits=([("qS", _c_ok), ("cV", B.n("cV"))] if hf == 0 else []),
                    inc=("cP" if hf == 1 else None))
            B.op("vector", lambda v: v.tensor_copy(R4[:, :], PS_E[:, :]),
                 waits=[("cP", B.n("cP"))], inc="cV")
            B.op("vector", (lambda o=out_sb: lambda v: v.scalar_tensor_tensor(
                o[:, :], PS_F[:, :], 1.0, R4[:, :], ALU.mult, ALU.add))(),
                inc="cV")

        _r(lambda v: v.tensor_copy(R1[:, :], LG[:, :]),
           [("qS", _lg_ok), ("qSL", B.n("qSL"))])
        first_max_onehot(R1, rs(0))
        _r(lambda v: v.tensor_copy(RB16[:, :], R1[:, :]))
        cumsum_block(R5, 0, 1, 0)
        _r(lambda v: v.tensor_mul(R4[:, :], R5[:, :], R1[:, :]))
        _r(lambda v: v.reduce_sum(rs(1), d3(R4), axis=AX))
        _r(lambda v: v.tensor_scalar(rs(2), rs(1), float(CAP), None, ALU.is_lt))
        B.op("tensor", lambda t: t.matmul(PS_C[0:1, 0:8], OCF[:, :], rc(0),
                                          start=True, stop=True),
             waits=[("cV", B.n("cV"))], inc="cP")
        B.op("vector", lambda v: v.tensor_copy(RCE[0:1, 16:24], PS_C[0:1, 0:8]),
             waits=[("cP", B.n("cP"))], inc="cV")
        _r(lambda v: v.tensor_scalar(RCE[0:1, 16:24], RCE[0:1, 16:24],
                                     float(CAP), None, ALU.min))
        B.op("tensor", lambda t: t.matmul(PS_C[:, 0:8], ORF[:, :],
                                          RCE[0:1, 16:24], start=True, stop=True),
             waits=[("cV", B.n("cV"))], inc="cP")
        B.op("vector", lambda v: v.tensor_copy(rc(3), PS_C[:, 0:8]),
             waits=[("cP", B.n("cP"))], inc="cV")
        _r(lambda v: v.scalar_tensor_tensor(R2[:, :], R1[:, :], -LGW, LG[:, :],
                                            ALU.mult, ALU.add))
        first_max_onehot(R2, rs(3))
        _r(lambda v: v.tensor_scalar(rs(4), rs(2), -1.0, 1.0, ALU.mult, ALU.add))
        _r(lambda v: v.tensor_tensor(d3(R2), d3(R2), bcl(rs(4)), ALU.mult))
        _r(lambda v: v.tensor_copy(RB16[:, :], R2[:, :]))
        cumsum_block(R5, 4, 5, 2)
        _r(lambda v: v.tensor_mul(R4[:, :], R5[:, :], R2[:, :]))
        _r(lambda v: v.reduce_sum(rs(5), d3(R4), axis=AX))
        _r(lambda v: v.tensor_tensor(
            d3(R4), d3(R2),
            rc(3).rearrange("p (one e) -> p one e", one=1)
                 .to_broadcast([128, 128, 8]), ALU.mult))
        _r(lambda v: v.reduce_sum(rs(6), d3(R4), axis=AX))
        _r(lambda v: v.tensor_add(rs(7), rs(6), rs(5)))
        _r(lambda v: v.tensor_scalar(rs(7), rs(7), float(CAP), None, ALU.is_lt))
        _r(lambda v: v.tensor_mul(rs(7), rs(7), rs(4)))
        _r(lambda v: v.tensor_tensor(d3(R3), d3(R1), bcl(rs(2)), ALU.mult))
        _r(lambda v: v.tensor_tensor(d3(R4), d3(R2), bcl(rs(7)), ALU.mult))
        _r(lambda v: v.tensor_add(R3[:, :], R3[:, :], R4[:, :]))
        _r(lambda v: v.tensor_sub(rs(8), rs(4), rs(7)))
        _r(lambda v: v.tensor_copy(RB16[:, :], R3[:, :]))
        cumsum_block(R5, 6, 7, 4)
        _r(lambda v: v.tensor_mul(R4[:, :], R5[:, :], R3[:, :]))
        _r(lambda v: v.reduce_sum(rs(9), d3(R4), axis=AX))
        B.op("gpsimd", lambda g: g.iota(
            R5[:, :], pattern=[[0, 128], [CAP, 8]], base=0, channel_multiplier=0,
            allow_small_or_imprecise_dtypes=True),
            waits=[("cV", B.n("cV"))], inc="cG")
        B.op("vector", lambda v: v.tensor_mul(R4[:, :], R3[:, :], R5[:, :]),
             waits=[("cG", B.n("cG"))], inc="cV")
        _r(lambda v: v.reduce_sum(rs(10), d3(R4), axis=AX))
        _r(lambda v: v.tensor_add(rs(10), rs(10), rs(9)))
        _r(lambda v: v.tensor_copy(RB16[:, 0:128], rs(8)))
        B.op("tensor", lambda t: t.matmul(PS_E[:, 0:128], TRB[:, :],
                                          RB16[:, 0:128], start=True, stop=True),
             waits=[("cV", B.n("cV"))], inc=None)
        B.op("tensor", lambda t: t.matmul(PS_F[0:1, 0:128], OCB[:, :],
                                          RB16[:, 0:128], start=True, stop=True),
             inc="cP")
        B.op("vector", lambda v: v.tensor_copy(R4[0:1, 0:128], PS_F[0:1, 0:128]),
             waits=[("cP", B.n("cP"))], inc="cV")
        dma("sync", bnc[6:7, 0:128], R4[0:1, 0:128], "qS",
            waits=[("cV", B.n("cV"))])
        _f_ok = dma("sync", RCE[:, 24:25], bnc[6:7, 0:128], "qS",
                    waits=[("qS", 0)])
        B.op("tensor", lambda t: t.matmul(PS_C[:, 0:1], TRF[:, :], RCE[:, 24:25],
                                          start=True, stop=True),
             waits=[("qS", _f_ok), ("cV", B.n("cV"))], inc="cP")
        B.op("vector", lambda v: v.tensor_copy(RCE[:, 25:26], PS_C[:, 0:1]),
             waits=[("cP", B.n("cP"))], inc="cV")
        dma("sync", bnc[7:8, 0:128], RCE[:, 25:26], "qS",
            waits=[("cV", B.n("cV"))])
        _f2_ok = dma("sync", R4[0:1, 0:128], bnc[7:8, 0:128], "qS",
                     waits=[("qS", 0)])
        B.op("tensor", lambda t: t.matmul(PS_F[:, 0:128], ORF[:, :],
                                          R4[0:1, 0:128], start=True, stop=True),
             waits=[("qS", _f2_ok), ("cV", B.n("cV"))], inc="cP")
        B.op("vector", lambda v: v.tensor_copy(rs(11), PS_E[:, 0:128]),
             waits=[("cP", B.n("cP"))], inc="cV")
        _r(lambda v: v.scalar_tensor_tensor(rs(11), PS_F[:, 0:128], 1.0, rs(11),
                                            ALU.mult, ALU.add))
        _r(lambda v: v.reduce_sum(rs(0), d3(R3), axis=AX))
        _r(lambda v: v.tensor_scalar(rs(11), rs(11), float(NT), None, ALU.add))
        _r(lambda v: v.tensor_mul(rs(10), rs(10), rs(0)))
        _r(lambda v: v.tensor_scalar(rs(1), rs(0), -1.0, 1.0, ALU.mult, ALU.add))
        _r(lambda v: v.tensor_mul(rs(11), rs(11), rs(1)))
        _r(lambda v: v.tensor_add(rs(10), rs(10), rs(11)))
        B.op("gpsimd", lambda g: g.iota(
            R5[:, :], pattern=[[0, 128], [FBPC, 8]], base=0,
            channel_multiplier=0, allow_small_or_imprecise_dtypes=True),
            waits=[("cV", B.n("cV"))], inc="cG")
        B.op("vector", lambda v: v.tensor_mul(R4[:, :], R3[:, :], R5[:, :]),
             waits=[("cG", B.n("cG"))], inc="cV")
        _r(lambda v: v.reduce_sum(rs(3), d3(R4), axis=AX))
        _r(lambda v: v.tensor_add(rs(3), rs(3), rs(9)))
        _r(lambda v: v.tensor_scalar(rs(5), rs(9), float(FBPC), None, ALU.is_ge))
        _r(lambda v: v.tensor_scalar(rs(5), rs(5), float(8192 - FBPC), None,
                                     ALU.mult))
        _r(lambda v: v.tensor_add(rs(3), rs(3), rs(5)))
        _r(lambda v: v.tensor_scalar(rs(6), rs(10), float(NT), None,
                                     ALU.subtract))
        _r(lambda v: v.tensor_scalar(rs(6), rs(6), 0.0, None, ALU.max))
        _r(lambda v: v.tensor_scalar(rs(6), rs(6), float(2 * 8192), None,
                                     ALU.add))
        _r(lambda v: v.tensor_scalar(rs(7), rs(10), float(NT), None, ALU.is_lt))
        _r(lambda v: v.tensor_mul(rs(3), rs(3), rs(7)))
        _r(lambda v: v.tensor_scalar(rs(8), rs(7), -1.0, 1.0, ALU.mult, ALU.add))
        _r(lambda v: v.tensor_mul(rs(6), rs(6), rs(8)))
        _r(lambda v: v.tensor_add(rs(3), rs(3), rs(6)))
        B.op("tensor", lambda t: t.transpose(PS_B[:, 0:128], rs(10), IDF[:, :]),
             waits=[("cV", B.n("cV"))], inc="cP")
        B.op("vector", lambda v: v.tensor_copy(rs(11), PS_B[:, 0:128]),
             waits=[("cP", B.n("cP"))], inc="cV")
        _r(lambda v: v.tensor_copy(IDXI[:, :], rs(11)))
        _ok1 = dma("sync", idxall.rearrange("(c p) one -> c p one", p=128),
                   IDXI.rearrange("p (c one) -> p c one", one=1), "qJ",
                   waits=[("cV", B.n("cV"))])
        B.op("tensor", lambda t: t.transpose(PS_B[:, 0:128], rs(3), IDF[:, :]),
             waits=[("cV", B.n("cV"))], inc="cP")
        B.op("vector", lambda v: v.tensor_copy(rs(11), PS_B[:, 0:128]),
             waits=[("cP", B.n("cP")), ("qJ", _ok1)], inc="cV")
        _r(lambda v: v.tensor_copy(IDXI[:, :], rs(11)))
        _ok2 = dma("sync", idxy.rearrange("(c p) one -> c p one", p=128),
                   IDXI.rearrange("p (c one) -> p c one", one=1), "qJ",
                   waits=[("cV", B.n("cV"))])

        B.op("gpsimd", lambda g: g.indirect_dma_start(
            out=OWNW[:, :], out_offset=None, in_=idxall[:, :],
            in_offset=IndirectOffsetOnAxis(IIX[:, 0:1], 0)),
            waits=[("qJ", _ok1)], inc="qI", inc_by=16)
        B.op("gpsimd", lambda g: g.indirect_dma_start(
            out=OWNY[:, :], out_offset=None, in_=idxy[:, :],
            in_offset=IndirectOffsetOnAxis(IIX[:, 0:1], 0)),
            waits=[("qJ", _ok2)], inc="qI", inc_by=16)
        _own_qI = B.n("qI")
        for j in range(16):
            B.op("gpsimd", (lambda j_=j: lambda g: g.indirect_dma_start(
                out=tbl_in[:, :],
                out_offset=IndirectOffsetOnAxis(OWNW[:, j_:j_ + 1], 0),
                in_=TIDS[:, j_:j_ + 1], in_offset=None,
                bounds_check=bc_regs[TBLN - 1], oob_is_err=False))(),
                waits=([("qI", _own_qI), ("qS", _pre_qS)] if j == 0 else []),
                inc="qI", inc_by=16)
        B.op("gpsimd", lambda g: g.collective_compute(
            "AllReduce", ALU.add, replica_groups=[list(range(NCORE))],
            ins=[tbl_in], outs=[tbl_all]),
            waits=[("qI", B.n("qI"))], inc="cc")
        _cc_tbl = B.n("cc")
        for col in range(24):
            B.op("gpsimd", (lambda c_=col: lambda g: g.indirect_dma_start(
                out=TBLW[:, c_:c_ + 1], out_offset=None, in_=tbl_all[:, :],
                in_offset=IndirectOffsetOnAxis(IIX2[:, c_:c_ + 1], 0)))(),
                waits=([("cc", _cc_tbl)] if col == 0 else []),
                inc="qI", inc_by=16)
        B.op("vector", lambda v: v.tensor_copy(TBLF[:, :], TBLW[:, :]),
             waits=[("qI", B.n("qI"))], inc="cV")
        B.op("vector", lambda v: v.tensor_scalar(
            TBLF[:, :], TBLF[:, :], 1.0, None, ALU.subtract), inc="cV")
        B.op("vector", lambda v: v.tensor_scalar(
            SMALL[:, 24:48], TBLF[:, :], 0.0, None, ALU.is_lt), inc="cV")
        B.op("vector", lambda v: v.scalar_tensor_tensor(
            TBLF[:, :], SMALL[:, 24:48], 9.9e5, TBLF[:, :], ALU.mult, ALU.add),
            inc="cV")
        B.op("vector", lambda v: v.tensor_copy(TBLW[:, :], TBLF[:, :]), inc="cV")
        _tblw_cV = B.n("cV")


        B.op("vector", lambda v: v.memset(SPREF[:, :], 0.0),
             waits=[("cc", _cc_sl)], inc="cV")
        for r in range(NCORE):
            src = sl_all[r * 1024:(r + 1) * 1024, :].rearrange(
                "(h dk) dv -> dk h dv", dk=64)
            _sl_ok = dma("sync",
                         XLP[0:64, :].rearrange("p (h dv) -> p h dv", dv=64),
                         src, "qS",
                         waits=[("cc", _cc_sl), ("cV", B.n("cV"))])
            _sl_ok = dma("sync",
                         XLP[64:128, :].rearrange("p (h dv) -> p h dv", dv=64),
                         src, "qS")
            B.op("vector", (lambda r_=r: lambda v: v.scalar_tensor_tensor(
                SPREF[:, :], XLP[:, :], PMB[:, r_:r_ + 1],
                SPREF[:, :], ALU.mult, ALU.add))(),
                waits=[("qS", _sl_ok)], inc="cV")
        B.op("vector", lambda v: v.tensor_copy(SOP[:, :], SPREF[:, :]),
             inc="cV")
        B.op("vector", lambda v: v.memset(SST[:, :], 0.0), inc="cV")
        _sop_cV = B.n("cV")
        _sst_cV = B.n("cV")

        spark_tiles = spark.rearrange("(c p) e -> c p e", p=128)
        etm_sl = lambda h: ETM[:, (h % 4) * 128:(h % 4) * 128 + 128]
        ps_free = {0: B.n("cV"), 1: B.n("cV")}
        etm_free = {i: 0 for i in range(4)}
        head_state = []

        def plan_et(c, h):
            par = h % 2
            ps_et = PS_A if par == 0 else PS_B
            kf = KFM[(h % 2) * 64:(h % 2) * 64 + 64,
                     (h // 2) * 2048 + c * 128:(h // 2) * 2048 + c * 128 + 128]
            qf = QFM[(h % 2) * 64:(h % 2) * 64 + 64,
                     (h // 2) * 2048 + c * 128:(h // 2) * 2048 + c * 128 + 128]
            B.op("tensor", (lambda p_=ps_et, kf_=kf, qf_=qf: lambda t:
                 t.matmul(p_[:, 0:128], kf_, qf_, start=True, stop=True))(),
                waits=[("cV", max(_q_cV, ps_free[par]))], inc="cP")
            B.op("vector", (lambda p_=ps_et, h_=h: lambda v: v.tensor_tensor(
                etm_sl(h_), p_[:, 0:128], MSK[:, :], ALU.mult))(),
                waits=[("cP", B.n("cP")), ("cP", etm_free[h % 4])], inc="cV")
            ps_free[par] = B.n("cV")
            head_state.append((c, h, qf, B.n("cV")))

        def plan_consume():
            c, h, qf, mask_cV = head_state.pop(0)
            B.op("tensor", (lambda c_=c, h_=h: lambda t: t.matmul(
                PS_E[:, h_ * 64:(h_ + 1) * 64], etm_sl(h_),
                VTM[:, c_ * 1024 + h_ * 64:c_ * 1024 + h_ * 64 + 64],
                start=True, stop=False))(),
                waits=[("cV", mask_cV)] +
                      ([("cV", ph4_og.get(c - 1, 0))] if h == 0 else []),
                inc=None)
            B.op("tensor", (lambda h_=h, qf_=qf, i_=h % 2: lambda t: t.matmul(
                PS_E[:, h_ * 64:(h_ + 1) * 64], qf_,
                SOP[i_ * 64:i_ * 64 + 64, h_ * 64:(h_ + 1) * 64],
                start=False, stop=True))(),
                waits=([("cV", ph4_sop[c])] if h == 0 else []), inc=None)
            B.op("tensor", (lambda c_=c, h_=h: lambda t: t.matmul(
                PS_F[0:64, h_ * 64:(h_ + 1) * 64],
                KTM[:, c_ * 1024 + h_ * 64:c_ * 1024 + h_ * 64 + 64],
                VTM[:, c_ * 1024 + h_ * 64:c_ * 1024 + h_ * 64 + 64],
                start=True, stop=True))(),
                waits=([("cV", ph4_sst[c])] if h == 0 else []), inc=None)
            B.op("tensor", (lambda c_=c, h_=h: lambda t: t.matmul(
                PS_F[64:128, h_ * 64:(h_ + 1) * 64],
                KTM[:, c_ * 1024 + h_ * 64:c_ * 1024 + h_ * 64 + 64],
                VTM[:, c_ * 1024 + h_ * 64:c_ * 1024 + h_ * 64 + 64],
                start=True, stop=True, tile_position=(0, 64)))(),
                inc="cP")
            etm_free[h % 4] = B.n("cP")

        ph4_sop = {0: _sop_cV}
        ph4_sst = {0: _sst_cV}
        ph4_og = {}
        epi_sig_cA = {}
        epi_sout_cV = {}

        epi_sts_cV = {}

        def plan_epi_A(c):
            spb = SPB0 if c % 2 == 0 else SPB1
            _spb_cV = epi_spb[c]
            for j in range(8):
                B.op("tensor", (lambda j_=j, s_=spb: lambda t: t.transpose(
                    PS_TB[:, j_ * 128:j_ * 128 + 128],
                    s_[:, j_ * 128:j_ * 128 + 128], IDBF[:, :]))(),
                    waits=([("cV", _spb_cV)] if j == 0 else []),
                    inc=("cP" if j == 7 else None))
            B.op("vector", lambda v: v.tensor_copy(STS[:, :], PS_TB[:, :]),
                 waits=[("cP", B.n("cP"))], inc="cV")
            epi_sts_cV[c] = B.n("cV")

        def plan_epi_B(c):
            _psc_cA = epi_sig_cA.get(c + 1, B.n("cA"))
            _psc_cV = max(_wof_cV, epi_sts_cV[c], epi_sout_cV.get(c - 1, 0))
            so_sl = (c % 2) * 1024
            for half in range(2):
                for j in range(8):
                    B.op("tensor", (lambda j_=j, h_=half: lambda t:
                         t.matmul(PS_C[:, 0:512], STS[:, j_ * 128:j_ * 128 + 128],
                                  WOB[:, j_ * 1024 + h_ * 512:
                                       j_ * 1024 + h_ * 512 + 512],
                                  start=(j_ == 0), stop=False))(),
                         waits=([("cA", _psc_cA), ("cV", _psc_cV)]
                                if j == 0 else []),
                         inc=None)
                B.op("tensor", (lambda h_=half: lambda t: t.matmul(
                    PS_C[:, 0:512], ORB[0:1, 0:128],
                    (BWOB0 if h_ == 0 else BWOB1)[0:1, :],
                    start=False, stop=True))(),
                    inc="cP")
                B.op("vector", (lambda h_=half, s_=so_sl: lambda v:
                     v.tensor_copy(SOUT[:, s_ + h_ * 512:s_ + h_ * 512 + 512],
                                   PS_C[:, 0:512]))(),
                     waits=[("cP", B.n("cP")), ("qA", B.n("qA"))], inc="cV")
                _psc_cV = B.n("cV")
            epi_sout_cV[c] = B.n("cV")
            dma("scalar", spark_tiles[c, :, :], SOUT[:, so_sl:so_sl + 1024],
                "qA", waits=[("cV", B.n("cV"))])

        epi_spb = {}
        heads_cP_hist = {}
        for c in range(NCH):
            if c >= 1:
                plan_epi_A(c - 1)
            for h in range(H):
                plan_et(c, h)
                if len(head_state) >= 2:
                    plan_consume()
            while head_state:
                plan_consume()
            _heads_cP = B.n("cP")
            heads_cP_hist[c] = _heads_cP
            if c % 4 == 3:
                for k2 in (c // 4, c // 4 + 4):
                    dma("gpsimd", W1B[:, k2 * 4096:(k2 + 1) * 4096],
                        w1m[k2 * 128:(k2 + 1) * 128, :], "qZ0",
                        waits=[("cP", _heads_cP)])
            if c < NCH - 1:
                B.op("vector", lambda v: v.scalar_tensor_tensor(
                    SST[:, :], PS_F[:, :], 1.0, SST[:, :],
                    ALU.mult, ALU.add),
                    waits=[("cP", _heads_cP)], inc="cV")
                ph4_sst[c + 1] = B.n("cV")
                B.op("vector", lambda v: v.tensor_add(
                    SOP[:, :], SST[:, :], SPREF[:, :]), inc="cV")
                ph4_sop[c + 1] = B.n("cV")
            for half in range(2):
                gw = ([("cV", B.n("cV")), ("cA", B.n("cA"))] if c < 2 else
                      [("cV", epi_sout_cV[c - 2])])
                B.op("tensor", (lambda c_=c, h_=half: lambda t: t.matmul(
                    PS_C[:, 0:512], G1FM[0:64, c_ * 128:c_ * 128 + 128],
                    WG2S[0:64, h_ * 512:(h_ + 1) * 512],
                    start=True, stop=True))(),
                    waits=(gw + [("qW1", wg2_ok)])
                        if half == 0 else [("cA", B.n("cA"))],
                    inc="cP")
                B.op("scalar", (lambda h_=half: lambda s: s.activation(
                    GATE[:, h_ * 512:(h_ + 1) * 512], PS_C[:, 0:512],
                    AF.Sigmoid))(),
                    waits=[("cP", B.n("cP"))], inc="cA")
            epi_sig_cA[c] = B.n("cA")

            B.op("vector", lambda v: v.tensor_tensor(OG[:, :], PS_E[:, :],
                                                     GATE[:, :], ALU.mult),
                 waits=[("cP", _heads_cP), ("cA", B.n("cA"))], inc="cV")
            ph4_og[c] = B.n("cV")
            if dump:
                dma("scalar", ogd.rearrange("(c p) e -> c p e", p=128)[c, :, :],
                    GATE[:, :], "qA", waits=[("cV", B.n("cV"))])
            B.op("vector", lambda v: v.reduce_sum(SMALL[:, 8:9], OG[:, :],
                                                  axis=AX), inc="cV")
            B.op("scalar", lambda s: s.activation(SPRE[:, :], OG[:, :],
                                                  AF.Square,
                                                  accum_out=SMALL[:, 9:10]),
                 waits=[("cV", B.n("cV")), ("qA", B.n("qA"))], inc="cA")
            B.op("vector", lambda v: v.tensor_scalar(
                SMALL[:, 10:11], SMALL[:, 8:9], 1.0 / E, None, ALU.mult),
                waits=[("cA", B.n("cA"))], inc="cV")
            B.op("vector", lambda v: v.tensor_mul(
                SMALL[:, 11:12], SMALL[:, 10:11], SMALL[:, 10:11]), inc="cV")
            B.op("vector", lambda v: v.tensor_scalar(
                SMALL[:, 12:13], SMALL[:, 9:10], 1.0 / E, None, ALU.mult),
                inc="cV")
            B.op("vector", lambda v: v.tensor_sub(
                SMALL[:, 12:13], SMALL[:, 12:13], SMALL[:, 11:12]), inc="cV")
            B.op("scalar", lambda s: s.activation(
                SMALL[:, 13:14], SMALL[:, 12:13], AF.Ln, bias=1e-5, scale=1.0),
                waits=[("cV", B.n("cV"))], inc="cA")
            B.op("scalar", lambda s: s.activation(
                SMALL[:, 14:15], SMALL[:, 13:14], AF.Exp, bias=0.0, scale=-0.5),
                inc="cA")
            B.op("vector", lambda v: v.tensor_scalar(
                SPRE[:, :], OG[:, :], SMALL[:, 10:11], SMALL[:, 14:15],
                ALU.subtract, ALU.mult),
                waits=[("cA", B.n("cA"))], inc="cV")
            spb = SPB0 if c % 2 == 0 else SPB1
            B.op("vector", (lambda s_=spb: lambda v: v.tensor_copy(
                s_[:, :], SPRE[:, :]))(),
                waits=[("cP", B.n("cP"))], inc="cV")
            epi_spb[c] = B.n("cV")
            if c >= 1:
                plan_epi_B(c - 1)
        plan_epi_A(NCH - 1)
        plan_epi_B(NCH - 1)
        _ph4_cP = B.n("cP")
        _ph4_cV = B.n("cV")

        _b2_ok = dma("sync", B2ROW[:, :], b2x, "qS",
                     waits=[("cV", _ph4_cV), ("cA", B.n("cA")),
                            ("qA", B.n("qA"))])
        for h0 in range(4):
            B.op("tensor", (lambda h_=h0: lambda t: t.matmul(
                PS_A[:, 0:512], ORF[:, :],
                B2ROW[0:1, h_ * 512:(h_ + 1) * 512], start=True, stop=True))(),
                waits=[("qS", _b2_ok), ("cV", B.n("cV")), ("cP", _ph4_cP)],
                inc="cP")
            B.op("vector", (lambda h_=h0: lambda v: v.tensor_copy(
                B2BC[:, h_ * 512:(h_ + 1) * 512], PS_A[:, 0:512]))(),
                waits=[("cP", B.n("cP"))], inc="cV")
        _b2bc_cV = B.n("cV")

        _h15 = heads_cP_hist[NCH - 1]
        for hf in range(2):
            dma("gpsimd",
                W2B[:, hf * 16384:(hf + 1) * 16384].rearrange(
                    "p (fc e) -> p fc e", e=1024),
                w2m[hf * 2048:(hf + 1) * 2048, :].rearrange(
                    "(fc p) e -> p fc e", p=128), "qZ1",
                waits=[("cP", _h15)])
        _w2m_qX1 = B.n("qZ1")
        _w1m_qX0 = B.n("qZ0")
        XG3 = XG.rearrange("p (kc n) -> p kc n", n=512)
        PT3 = PS_TB.rearrange("p (kc n) -> p kc n", n=128)
        nblocks = [(nb, "e") for nb in range(4)] + [(nb, "f") for nb in range(2)]

        def plan_gather_dma(bi):
            last = 0
            for m in range(4):
                col = bi * 4 + m
                xgr_sl = m * 1024
                B.op("gpsimd", (lambda s_=xgr_sl: lambda g: g.memset(
                    XGR[:, s_:s_ + 1024], 0.0))(),
                    waits=[("cV", _tblw_cV), ("cP", B.n("cP")),
                           ("qI", B.n("qI")), ("qA", B.n("qA"))], inc="cG")
                B.op("gpsimd", (lambda s_=xgr_sl, c_=col: lambda g:
                     g.indirect_dma_start(
                         out=XGR[:, s_:s_ + 1024], out_offset=None,
                         in_=xnall[:, :],
                         in_offset=IndirectOffsetOnAxis(TBLW[:, c_:c_ + 1], 0),
                         bounds_check=bc_regs[NT - 1], oob_is_err=False))(),
                    waits=[("cG", B.n("cG")), ("cc", _cc_xn)],
                    inc="qI", inc_by=16)
                last = B.n("qI")
            return last

        def plan_gather_pe(bi, qig):
            for m in range(4):
                xgr_sl = m * 1024
                for kc in range(8):
                    B.op("tensor", (lambda s_=xgr_sl, k_=kc: lambda t: t.transpose(
                        PT3[:, k_, :], XGR[:, s_ + k_ * 128:s_ + k_ * 128 + 128],
                        IDBF[:, :]))(),
                        waits=(([("qI", qig)] if m == 0 else []) +
                               [("cV", B.n("cV"))] if kc == 0 else []),
                        inc=("cP" if kc == 7 else None))
                B.op("vector", (lambda m_=m: lambda v: v.tensor_copy(
                    XG3[:, :, m_ * 128:m_ * 128 + 128], PT3[:, :, :]))(),
                    waits=[("cP", B.n("cP"))], inc="cV")
            return B.n("cV")

        _g0 = plan_gather_dma(0)
        xg_ready = {0: plan_gather_pe(0, _g0)}
        sw_ok = {}
        PSY = (PS_A, PS_B, PS_E, PS_F)
        _cc_y = None
        yev01 = {}
        for bi, (nb, kind) in enumerate(nblocks):
            _xg_cV = xg_ready[bi]
            _w1_free = _b2bc_cV if bi == 0 else yev01[bi - 1]
            bcol = 0 if kind == "e" else 32
            w1w = ([("qZ0", _w1m_qX0)] if bi == 0 else
                   ([("qZ0", sw_ok["w1"])] if (kind == "f" and nb == 0) else []))
            gelu_hist = {}
            for ft in range(32):
                ps_h = PS_C if ft % 2 == 0 else PS_A
                w_g = gelu_hist.get(ft - 2, B.n("cA") if ft < 2 else 0)
                for kc in range(8):
                    B.op("tensor", (lambda p_=ps_h, f_=ft, k_=kc:
                         lambda t: t.matmul(
                             p_[:, 0:512],
                             W1B[:, k_ * 4096 + f_ * 128:
                                  k_ * 4096 + f_ * 128 + 128],
                             XG[:, k_ * 512:(k_ + 1) * 512],
                             start=(k_ == 0), stop=(k_ == 7)))(),
                         waits=([("cV", max(_xg_cV, _w1_free)),
                                 ("cA", w_g)] +
                                (w1w if ft == 0 else [])
                                if kc == 0 else []),
                         inc=("cP" if kc == 7 else None))
                B.op("scalar", (lambda p_=ps_h, f_=ft, b_=bcol: lambda s:
                     s.activation(
                         HSB[:, f_ * 512:(f_ + 1) * 512], p_[:, 0:512], AF.Gelu,
                         bias=B1SS[:, b_ + f_:b_ + f_ + 1]))(),
                    waits=[("cP", B.n("cP"))], inc="cA")
                gelu_hist[ft] = B.n("cA")
            _h_cA = B.n("cA")
            if bi == 3:
                for k2 in range(8):
                    dma("gpsimd", W1B[:, k2 * 4096:(k2 + 1) * 4096],
                        sw1d[k2 * 128:(k2 + 1) * 128, :], "qZ0",
                        waits=([("cP", B.n("cP"))] if k2 == 0 else []) +
                              [("qG", _swd_qG)])
                sw_ok["w1"] = B.n("qZ0")
            b2c = 0 if kind == "e" else E
            qy = "qYA" if (kind == "e" and nb < 2) else (
                "qYB" if kind == "e" else "qYC")
            w2w = ([("qZ1", _w2m_qX1)] if bi == 0 else
                   ([("qZ1", sw_ok["w2"])] if (kind == "f" and nb == 0) else []))
            for half in range(2):
                for fc in range(32):
                    for mt in range(4):
                        B.op("tensor", (lambda p_=PSY[mt], f_=fc, m_=mt,
                                        h_=half: lambda t: t.matmul(
                            p_[:, 0:512],
                            HSB[:, f_ * 512 + m_ * 128:f_ * 512 + m_ * 128 + 128],
                            W2B[:, f_ * 1024 + h_ * 512:
                                 f_ * 1024 + h_ * 512 + 512],
                            start=(f_ == 0), stop=(f_ == 31)))(),
                            waits=([("cA", _h_cA), ("cV", B.n("cV"))] + w2w
                                   if (fc == 0 and mt == 0) else []),
                            inc=("cP" if (fc % 8 == 7 and mt == 3) else None))
                    if half == 1 and bi + 1 < len(nblocks):
                        if fc == 7:
                            gq = plan_gather_dma(bi + 1)
                        elif fc == 23:
                            xg_ready[bi + 1] = plan_gather_pe(bi + 1, gq)
                _y_cP = B.n("cP")
                for mt in range(4):
                    yev_sl = (mt % 2) * 1024
                    B.op("vector", (lambda p_=PSY[mt], y_=yev_sl, h_=half,
                                    b_=b2c: lambda v: v.scalar_tensor_tensor(
                        YEV[:, y_ + h_ * 512:y_ + h_ * 512 + 512],
                        p_[:, 0:512], 1.0,
                        B2BC[:, b_ + h_ * 512:b_ + h_ * 512 + 512],
                        ALU.mult, ALU.add))(),
                        waits=[("cP", _y_cP), (qy, B.n(qy)),
                               ("cV", max(_b2bc_cV, B.n("cV")))], inc="cV")
                    if half == 1 and mt == 0:
                        yev01[bi] = B.n("cV")
                    base = (0 if kind == "e" else CAP) + nb * 512 + mt * 128
                    dma("gpsimd",
                        ymine[base:base + 128, half * 512:(half + 1) * 512],
                        YEV[:, yev_sl + half * 512:yev_sl + half * 512 + 512],
                        qy, waits=[("cV", B.n("cV"))])
            if bi == 3:
                for hf in range(2):
                    dma("gpsimd",
                        W2B[:, hf * 16384:(hf + 1) * 16384].rearrange(
                            "p (fc e) -> p fc e", e=1024),
                        sw2d[hf * 2048:(hf + 1) * 2048, :].rearrange(
                            "(fc p) e -> p fc e", p=128), "qZ1",
                        waits=([("cP", B.n("cP"))] if hf == 0 else []) +
                              [("qG", _swd_qG)])
                sw_ok["w2"] = B.n("qZ1")
            if bi == 1:
                B.op("gpsimd", lambda g: g.collective_compute(
                    "AllGather", ALU.bypass,
                    replica_groups=[list(range(NCORE))],
                    ins=[ymine[0:1024, :]], outs=[yall[0:8192, :]]),
                    waits=[("qYA", B.n("qYA"))], inc="cc")
            if bi == 3:
                B.op("gpsimd", lambda g: g.collective_compute(
                    "AllGather", ALU.bypass,
                    replica_groups=[list(range(NCORE))],
                    ins=[ymine[1024:2048, :]], outs=[yall[8192:16384, :]]),
                    waits=[("qYB", B.n("qYB"))], inc="cc")
            if bi == 5:
                B.op("gpsimd", lambda g: g.collective_compute(
                    "AllGather", ALU.bypass,
                    replica_groups=[list(range(NCORE))],
                    ins=[ymine[2048:3072, :]], outs=[yall[16384:24576, :]]),
                    waits=[("qYC", B.n("qYC"))], inc="cc")
        _cc_y = B.n("cc")

        x_w = x_in.rearrange("(p w) e -> w p e", w=16)
        s_w = spark.rearrange("(p w) e -> w p e", w=16)
        o_w = out_t.rearrange("(p w) e -> w p e", w=16)
        _ph5_cV = B.n("cV")
        ph6_v = {}
        ph6_o = {}
        for j in range(16):
            sl2 = slice((j % 2) * 1024, (j % 2) * 1024 + 1024)
            if j >= 2:
                wgate = [("cV", ph6_v[j - 2]), ("qO", ph6_o[j - 2])]
            else:
                wgate = [("cV", _ph5_cV)]
            B.op("vector", (lambda s_=sl2: lambda v: v.memset(YGB[:, s_], 0.0))(),
                 waits=wgate + [("cc", _cc_y), ("cA", B.n("cA"))], inc="cV")
            B.op("gpsimd", (lambda j_=j, s_=sl2: lambda g: g.indirect_dma_start(
                out=YGB[:, s_], out_offset=None, in_=yall[:, :],
                in_offset=IndirectOffsetOnAxis(OWNY[:, j_:j_ + 1], 0),
                bounds_check=bc_regs[NCORE * ROWS - 1], oob_is_err=False))(),
                waits=[("cV", B.n("cV"))], inc="qI", inc_by=16)
            _yg_qI = B.n("qI")
            qx6 = "qX0" if j % 2 == 0 else "qX1"
            qd6 = "qD0" if j % 2 == 0 else "qD1"
            _x_ok = dma("sync", XLD[:, sl2], x_w[j], qx6, waits=wgate)
            _s_ok = dma("scalar", SLD[:, sl2], s_w[j], qd6,
                        waits=wgate + [("qA", B.n("qA"))])
            B.op("scalar", (lambda s_=sl2: lambda s: s.activation(
                OUTT[:, s_], YGB[:, s_], AF.Silu))(),
                waits=[("qI", _yg_qI), ("cA", B.n("cA"))], inc="cA")
            B.op("vector", (lambda s_=sl2: lambda v: v.tensor_mul(
                OUTT[:, s_], OUTT[:, s_], SLD[:, s_]))(),
                waits=[("cA", B.n("cA")), (qd6, _s_ok)], inc="cV")
            B.op("vector", (lambda s_=sl2: lambda v: v.tensor_add(
                OUTT[:, s_], OUTT[:, s_], XLD[:, s_]))(),
                waits=[(qx6, _x_ok)], inc="cV")
            ph6_v[j] = B.n("cV")
            B.op("sync", (lambda j_=j, s_=sl2: lambda e: e.dma_start(
                out=o_w[j_], in_=OUTT[:, s_]))(),
                waits=[("cV", B.n("cV"))], inc="qO", inc_by=16)
            ph6_o[j] = B.n("qO")

        for nm, src in (("d_xnall", xnall), ("d_lgall", lg_all),
                        ("d_slall", sl_all), ("d_tbl", tbl_all),
                        ("d_yall", yall)):
            if nm in dbgd:
                B.op("sync", (lambda d=dbgd[nm], s=src: lambda e: e.dma_start(
                    out=d, in_=s))(),
                    waits=[("cc", B.n("cc")), ("qO", B.n("qO"))],
                    inc="qO", inc_by=16)

        B.op("sync", lambda e: e.nop(), waits=[("qO", B.n("qO"))], inc=None)

        with nc.Block() as block:
            def mk(name):
                def body(eng):
                    if name == "gpsimd":
                        for v_ in (NT - 1, TBLN - 1, NCORE * ROWS - 1):
                            if v_ not in bc_regs:
                                bc_regs[v_] = eng.to_reg(v_)
                    for waits, fn, inc, inc_by in B.q[name]:
                        for s_, v_ in waits:
                            if v_ > 0:
                                eng.wait_ge(semh[s_], v_)
                        inst = fn(eng)
                        if inc is not None:
                            inst.then_inc(semh[inc], inc_by)
                return body
            block.sync(mk("sync"))
            block.scalar(mk("scalar"))
            block.vector(mk("vector"))
            block.tensor(mk("tensor"))
            block.gpsimd(mk("gpsimd"))

    return nc


def _host_inputs(inputs):
    x = np.ascontiguousarray(
        np.asarray(inputs["L0_attn_x"]).reshape(NT, E)).astype(np.float32)
    Wq = np.ascontiguousarray(np.asarray(inputs["L0_attn_Wq"], dtype=np.float32))
    Wk = np.ascontiguousarray(np.asarray(inputs["L0_attn_Wk"], dtype=np.float32))
    Wv = np.ascontiguousarray(np.asarray(inputs["L0_attn_Wv"], dtype=np.float32))
    Wo = np.ascontiguousarray(np.asarray(inputs["L0_attn_Wo"], dtype=np.float32))
    perm = np.zeros(E, np.int64)
    for h in range(H):
        perm[h * 64:h * 64 + 32] = np.arange(h * 64 + 32, h * 64 + 64)
        perm[h * 64 + 32:h * 64 + 64] = np.arange(h * 64, h * 64 + 32)
    Wqp = np.ascontiguousarray(Wq[:, perm])
    Wkp = np.ascontiguousarray(Wk[:, perm])
    inv_freq = (10000.0 ** (-(np.arange(D // 2, dtype=np.float64) * 2.0) / D)
                ).astype(np.float32)
    gam = np.asarray(inputs["L0_attn_gamma"], dtype=np.float32)
    bet = np.asarray(inputs["L0_attn_beta"], dtype=np.float32)
    gkc = np.ascontiguousarray(gam.reshape(8, 128).T)
    bkc = np.ascontiguousarray(bet.reshape(8, 128).T)
    in_maps = []
    p = np.arange(128, dtype=np.int32)
    for core in range(NCORE):
        t0 = core * T
        pos0 = t0 % 8192
        pos = np.arange(pos0, pos0 + T, dtype=np.float32)
        ang = (pos[:, None] * inv_freq[None, :]).astype(np.float32)
        csin = np.ascontiguousarray(np.concatenate(
            [np.cos(ang).astype(np.float32).T,
             -np.sin(ang).astype(np.float32).T,
             np.sin(ang).astype(np.float32).T], axis=0))
        pm = np.zeros((1, 8), np.float32)
        for r in range((core // 4) * 4, core):
            pm[0, r] = 1.0
        iix = np.zeros((128, 8), np.int32)
        iix[:, 0] = t0 + p * 16
        iix2 = np.zeros((128, 24), np.int32)
        for bi in range(4):
            for m in range(4):
                iix2[:, bi * 4 + m] = core * CAP + bi * 512 + m * 128 + p
        for bi in range(2):
            for m in range(4):
                iix2[:, 16 + bi * 4 + m] = (NT + core * FBPC + bi * 512 +
                                            m * 128 + p)
        tid = np.zeros((128, 16), np.int32)
        for j in range(16):
            tid[:, j] = t0 + p * 16 + j + 1
        b1v = np.asarray(inputs["L0_ffn_b1"], dtype=np.float32)[core]
        sb1v = np.asarray(inputs["L0_ffn_sb1"], dtype=np.float32)
        b1mat = np.ascontiguousarray(np.concatenate(
            [b1v.reshape(32, 128).T, sb1v.reshape(32, 128).T], axis=1))
        b2row = np.ascontiguousarray(np.concatenate(
            [np.asarray(inputs["L0_ffn_b2"], dtype=np.float32)[core],
             np.asarray(inputs["L0_ffn_sb2"], dtype=np.float32)]).reshape(1, 2 * E))
        in_maps.append({
            "x": np.ascontiguousarray(x[t0:t0 + T]),
            "wq": Wq, "wk": Wk, "wv": Wv, "wqp": Wqp, "wkp": Wkp, "wo": Wo,
            "wg1": np.ascontiguousarray(
                np.asarray(inputs["L0_attn_Wg1"], dtype=np.float32)),
            "wg2": np.ascontiguousarray(
                np.asarray(inputs["L0_attn_Wg2"], dtype=np.float32)),
            "gkc": gkc, "bkc": bkc,
            "rwt": np.ascontiguousarray(
                np.asarray(inputs["L0_ffn_rw"], dtype=np.float32).T),
            "rb": np.asarray(inputs["L0_ffn_rb"], dtype=np.float32).reshape(1, 8),
            "w1m": np.ascontiguousarray(
                np.asarray(inputs["L0_ffn_w1"], dtype=np.float32)[core]),
            "w2m": np.ascontiguousarray(
                np.asarray(inputs["L0_ffn_w2"], dtype=np.float32)[core]),
            "sw1": np.ascontiguousarray(
                np.asarray(inputs["L0_ffn_sw1"], dtype=np.float32)),
            "sw2": np.ascontiguousarray(
                np.asarray(inputs["L0_ffn_sw2"], dtype=np.float32)),
            "b1s": b1mat, "b2x": b2row, "csin": csin, "pmask": pm,
            "intidx": iix, "intidx2": iix2, "tids": tid,
        })
    return in_maps


def kernel(**inputs):
    in_maps = _host_inputs(inputs)
    nc = _build()
    res = run_bass_kernel_spmd(nc, in_maps, core_ids=list(range(NCORE)))
    out = np.concatenate([res.results[c]["out"] for c in range(NCORE)], axis=0)
    return np.ascontiguousarray(out.reshape(2, 8192, E))
